# version 12
# baseline (speedup 1.0000x reference)
import contextlib
import numpy as np
import ml_dtypes
import concourse.bass as bass
import concourse.mybir as mybir
from concourse.bass_utils import run_bass_kernel_spmd

F32, BF16 = mybir.dt.float32, mybir.dt.bfloat16
AF, ALU, AX = mybir.ActivationFunctionType, mybir.AluOpType, mybir.AxisListType.X

HID, S, NH, DIM, NE, NI = 2048, 4096, 16, 64, 64, 512
TOPK = 8
EPS_LN, EPS_AT = 1e-5, 1e-6
BIG = 1e30
NKC = HID // 128
NT = 8
NTB = 8

ENGS = ("sync", "gpsimd", "vector", "scalar", "tensor")


class Sem:
    __slots__ = ("h", "n")
    def __init__(self, h):
        self.h, self.n = h, 0


class Phase:
    def __init__(self):
        self.q = {e: [] for e in ENGS}

    def emit(self, eng, f):
        self.q[eng].append(f)

    def run(self, block):
        for name in ENGS:
            fns = self.q[name]
            if fns:
                getattr(block, name)(lambda E, fns=fns: [f(E) for f in fns])


def build_nc(debug=False, upto=9, ntb=NTB, nh=NH, ne=NE):
    nc = bass.Bass("TRN2", target_bir_lowering=False, debug=False)
    dt = lambda n, s, k="ExternalInput", d=F32: nc.dram_tensor(n, s, d, kind=k)

    x_full = dt("x_full", [S, HID]).ap()
    x_own = dt("x_own", [1024, HID]).ap()
    wq_img = dt("wq_img", [NH, 128, NKC, 128]).ap()
    wk_img = dt("wk_img", [NH, 128, NKC, 128]).ap()
    wv_img = dt("wv_img", [NH, 128, NKC, 128]).ap()
    wo_img = dt("wo_img", [4, 128, NKC, 512]).ap()
    bq_r = dt("bq_r", [128, NH]).ap()
    bk_r = dt("bk_r", [128, NH]).ap()
    bv_r = dt("bv_r", [128, NH]).ap()
    bo_bc = dt("bo_bc", [128, HID]).ap()
    wmap_q = dt("wmap_qT", [128, DIM]).ap()
    wmap_k = dt("wmap_kT", [128, DIM]).ap()
    gate_img = dt("gate_img", [128, NKC, NE]).ap()
    gu_img = dt("gu_img", [NE, 4, 128, NKC, 256]).ap()
    dp_img = dt("dp_img", [NE, 128, 4, HID]).ap()
    shg_img = dt("shg_img", [128, NKC, NI]).ap()
    shu_img = dt("shu_img", [128, NKC, NI]).ap()
    shd_img = dt("shd_img", [128, 4, HID]).ap()
    shg1_img = dt("shg1_img", [128, NKC, 1]).ap()
    ident_in = dt("ident_bf", [128, 128], d=BF16).ap()
    identf_in = dt("ident_f", [128, 128]).ap()
    out_t = dt("out", [1024, HID], k="ExternalOutput").ap()
    dbg = {}
    if debug:
        dbg["xn"] = dt("d_xn", [128, NKC, 1024], k="ExternalOutput", d=BF16).ap()
        dbg["kv"] = dt("d_kv", [64, NH, 128], k="ExternalOutput", d=BF16).ap()
        dbg["ksum"] = dt("d_ksum", [64, NH], k="ExternalOutput", d=BF16).ap()
        dbg["attn"] = dt("d_attn", [128, NKC, 1024], k="ExternalOutput", d=BF16).ap()
        dbg["accw"] = dt("d_accw", [128, NT, HID], k="ExternalOutput").ap()
        dbg["W"] = dt("d_W", [128, NT, NE], k="ExternalOutput").ap()
        dbg["shg"] = dt("d_shg", [128, NT], k="ExternalOutput").ap()

    est = contextlib.ExitStack()
    with est:
        sb = lambda n, s, d: est.enter_context(nc.sbuf_tensor(n, s, d))
        mksem = lambda n: Sem(est.enter_context(nc.semaphore(n)))

        xn_own = sb("xn_own", [128, NKC, 1024], BF16)
        kv_bf = sb("kv_bf", [64, NH, 128], BF16)
        ksum_bf = sb("ksum_bf", [64, NH], BF16)
        W_sb = sb("W_sb", [128, NT, NE], F32)
        shgate_sb = sb("shgate_sb", [128, NT], F32)
        gateT_bf = sb("gateT_bf", [128, NKC, NE], BF16)
        shg1_bf = sb("shg1_bf", [128, NKC, 1], BF16)
        wmq_bf = sb("wmq_bf", [128, DIM], BF16)
        wmk_bf = sb("wmk_bf", [128, DIM], BF16)
        bq_sb = sb("bq_sb", [128, NH], F32)
        bk_sb = sb("bk_sb", [128, NH], F32)
        bv_sb = sb("bv_sb", [128, NH], F32)
        ident = sb("ident", [128, 128], BF16)
        ones_bf = sb("ones_bf", [128, 1], BF16)
        iota_sb = sb("iota_sb", [128, NE], F32)
        lnsc = sb("lnsc", [128, 8], F32)
        rt_sc = sb("rt_sc", [128, 8], F32)
        rn_sc = sb("rn_sc", [128, NH + 2], F32)

        s_x = mksem("s_x")
        s_w = mksem("s_w")
        s_xp = [mksem("s_xp0"), mksem("s_xp1")]
        s_wk = [mksem("s_wk0"), mksem("s_wk1"), mksem("s_wk2")]
        s_wv = [mksem("s_wv0"), mksem("s_wv1"), mksem("s_wv2")]
        s_wq = [mksem("s_wq0"), mksem("s_wq1"), mksem("s_wq2")]
        s_wo2 = [mksem("s_wo0"), mksem("s_wo1")]
        s_gu = [mksem("s_gu0"), mksem("s_gu1")]
        s_dp = [mksem("s_dp0"), mksem("s_dp1")]
        s_op = [mksem("s_op0"), mksem("s_op1")]
        s_dbg = mksem("s_dbg")
        s_t = mksem("s_t")
        s_v = mksem("s_v")
        s_s = mksem("s_s")
        s_o = mksem("s_o")
        s_ph = mksem("s_ph")

        block = est.enter_context(nc.Block())

        def dma(p, eng, sem, out, in_):
            p.emit(eng, lambda E, out=out, in_=in_: E.dma_start(out=out, in_=in_).then_inc(sem.h, 16))
            sem.n += 16
            return sem.n

        def wait(p, eng, sem, val=None):
            v = sem.n if val is None else val
            if v > 0:
                p.emit(eng, lambda E, v=v: E.wait_ge(sem.h, v))

        def inc(p, eng, sem, f):
            p.emit(eng, lambda E, f=f: f(E).then_inc(sem.h, 1))
            sem.n += 1
            return sem.n

        def op(p, eng, f):
            p.emit(eng, f)

        def barrier(p):
            for e in ENGS:
                p.emit(e, lambda E: E.nop().then_inc(s_ph.h, 1))
            s_ph.n += len(ENGS)
            for e in ENGS:
                wait(p, e, s_ph)

        def seq(p, eng, builders):
            sem = {"vector": s_v, "scalar": s_s, "tensor": s_t}[eng]
            last = 0
            for b in builders:
                last = inc(p, eng, sem, b)
                wait(p, eng, sem, last)
            return last

        def phase_sync(p):
            for e in ("tensor", "vector", "scalar"):
                for sem in (s_v, s_s, s_t):
                    wait(p, e, sem)

        def run(p):
            p.run(block)

        T = dict(trps_free=0, xn_free=0)

        def emit_ln(p, x_tile, xn_bf_out):
            ln_xsq = T["xsq"]
            ventry = s_v.n
            seq(p, "vector", [
                lambda E: E.reduce_sum(lnsc[:, 0:1], x_tile, axis=AX),
                lambda E: E.tensor_scalar(lnsc[:, 1:2], lnsc[:, 0:1], 1.0 / HID, None, ALU.mult),
            ])
            wait(p, "scalar", s_v, ventry)
            g_sq = inc(p, "scalar", s_s, lambda E: E.activation(ln_xsq[:], x_tile, AF.Square))
            wait(p, "vector", s_s, g_sq)
            g_rv = seq(p, "vector", [
                lambda E: E.reduce_sum(lnsc[:, 2:3], ln_xsq[:], axis=AX),
                lambda E: E.tensor_scalar(lnsc[:, 3:4], lnsc[:, 2:3], 1.0 / HID, None, ALU.mult),
                lambda E: E.tensor_mul(lnsc[:, 4:5], lnsc[:, 1:2], lnsc[:, 1:2]),
                lambda E: E.tensor_sub(lnsc[:, 5:6], lnsc[:, 3:4], lnsc[:, 4:5]),
                lambda E: E.tensor_scalar(lnsc[:, 5:6], lnsc[:, 5:6], EPS_LN, None, ALU.add),
                lambda E: E.reciprocal(lnsc[:, 6:7], lnsc[:, 5:6]),
            ])
            wait(p, "scalar", s_v, g_rv)
            g_st = inc(p, "scalar", s_s, lambda E: E.activation(lnsc[:, 7:8], lnsc[:, 6:7], AF.Sqrt))
            wait(p, "vector", s_s, g_st)
            gv = inc(p, "vector", s_v, lambda E: E.tensor_scalar(
                xn_bf_out, x_tile, lnsc[:, 1:2], lnsc[:, 7:8], ALU.subtract, ALU.mult))
            return gv, g_st

        def emit_t16(p, trps, src_bf, dst_col_fn, idn=None):
            if idn is None:
                idn = ident
            B = trps.shape[1] // 128
            nb = NKC // B
            for batch in range(nb):
                wait(p, "tensor", s_v, T["trps_free"])
                for j in range(B - 1):
                    kc = batch * B + j
                    op(p, "tensor", lambda E, kc=kc, j=j: E.transpose(
                        trps[:, j * 128:(j + 1) * 128], src_bf[:, kc * 128:(kc + 1) * 128], idn[:]))
                kc = batch * B + B - 1
                gt = inc(p, "tensor", s_t, lambda E, kc=kc, B=B: E.transpose(
                    trps[:, (B - 1) * 128:B * 128], src_bf[:, kc * 128:(kc + 1) * 128], idn[:]))
                wait(p, "vector", s_t, gt)
                dst = dst_col_fn(batch, B)
                T["trps_free"] = inc(p, "vector", s_v, lambda E, dst=dst, B=B: E.tensor_copy(
                    dst, trps[:, 0:B * 128].rearrange("p (a b) -> p a b", a=B)))

        p0 = Phase()
        dma(p0, "sync", s_x, bq_sb[:], bq_r)
        dma(p0, "sync", s_x, bk_sb[:], bk_r)
        dma(p0, "sync", s_x, bv_sb[:], bv_r)
        dma(p0, "sync", s_x, ident[:], ident_in)
        dma(p0, "gpsimd", s_w, wmq_bf[:], wmap_q)
        dma(p0, "gpsimd", s_w, wmk_bf[:], wmap_k)
        dma(p0, "gpsimd", s_w, gateT_bf[:], gate_img)
        dma(p0, "gpsimd", s_w, shg1_bf[:], shg1_img)
        op(p0, "gpsimd", lambda E: E.iota(iota_sb[:], pattern=[[1, NE]], base=0,
                                          channel_multiplier=0,
                                          allow_small_or_imprecise_dtypes=True))
        op(p0, "vector", lambda E: E.memset(ones_bf[:], 1.0))
        for e in ("vector", "scalar", "tensor"):
            wait(p0, e, s_x)
            wait(p0, e, s_w)
        barrier(p0)
        run(p0)

        if upto >= 1:
          with contextlib.ExitStack() as ph:
            psb = lambda n, s, d: ph.enter_context(nc.psum_tensor(n, s, d))
            lsb = lambda n, s, d: ph.enter_context(nc.sbuf_tensor(n, s, d))
            kv_ps = psb("kv_ps", [64, NH, 128], F32)
            trps = psb("trps", [128, 1024], BF16)
            small2 = psb("small2", [128, 512], F32)
            k_ps = psb("k_ps", [128, 512], F32)
            v_ps = psb("v_ps", [128, 512], F32)
            k2_ps = small2[:, 0:256]
            kst = small2[0:64, 256:257]

            x_t2 = lsb("x_t2", [128, 2, HID], F32)
            xn_cast = lsb("xn_cast", [128, 2, HID], BF16)
            v_sb = lsb("v_sb", [128, 2, 512], BF16)
            ksum_acc = lsb("ksum_acc", [64, NH], F32)
            T["xsq"] = lsb("ln_xsq1", [128, HID], F32)
            xn_tb = lsb("xn_tb", [128, 2, NKC, 512], BF16)
            wk_sl = lsb("wk_sl", [128, 3, NKC, 128], BF16)
            wv_sl = lsb("wv_sl", [128, 3, NKC, 128], BF16)
            k_sb = lsb("k_sb", [128, 2, 512], BF16)
            v_tm = lsb("v_tm", [128, 2, 4, 128], BF16)
            k2_sb = lsb("k2_sb", [128, 2, 4, DIM], BF16)
            et = lsb("et", [128, 3, 256], F32)

            p1 = Phase()
            phase_sync(p1)
            kv_started = [False] * 4
            xt_free = [(0, 0), (0, 0)]
            wk_free = [0, 0, 0]
            wv_free = [0, 0, 0]
            XN_READY = [0, 0]
            XN_FREE = [0, 0]
            xchunk = [0]
            H = {}

            XLD = {}
            XLN = {}
            xc_free = [0, 0]
            lnchunk = [0]

            def emit_x_load(tb, ch, own=False):
                bi = xchunk[0] % 2
                xchunk[0] += 1
                fv, fs = xt_free[bi]
                wait(p1, "sync", s_v, fv)
                wait(p1, "sync", s_s, fs)
                r0 = ch * 128 if own else tb * 512 + ch * 128
                src = x_own if own else x_full
                XLD[(tb, ch, own)] = (bi, dma(p1, "sync", s_xp[bi], x_t2[:, bi, :], src[r0:r0 + 128, :]))

            def emit_ln_part(tb, ch, own=False):
                bi, ld = XLD.pop((tb, ch, own))
                cb = lnchunk[0] % 2
                lnchunk[0] += 1
                wait(p1, "vector", s_xp[bi], ld)
                wait(p1, "scalar", s_xp[bi], ld)
                wait(p1, "vector", s_t, xc_free[cb])
                gv, gs = emit_ln(p1, x_t2[:, bi, :], xn_cast[:, cb, :])
                xt_free[bi] = (gv, gs)
                XLN[(tb, ch, own)] = (cb, gv)

            def emit_trps_part(tb, ch, own=False):
                par = tb % 2
                cb, gv = XLN.pop((tb, ch, own))
                if ch == 0 and not own:
                    wait(p1, "vector", s_t, XN_FREE[par])
                wait(p1, "tensor", s_v, gv)
                if own:
                    dst = lambda batch, B, ch=ch: xn_own[:, batch * B:(batch + 1) * B,
                                                        ch * 128:(ch + 1) * 128]
                else:
                    dst = lambda batch, B, ch=ch, par=par: xn_tb[:, par, batch * B:(batch + 1) * B,
                                                                 ch * 128:(ch + 1) * 128]
                emit_t16(p1, trps[:, 0:1024] if False else trps, xn_cast[:, cb, :], dst)
                xc_free[cb] = s_t.n

            def emit_stageA_chunk(tb, ch):
                emit_x_load(tb, ch)
                emit_ln_part(tb, ch)
                emit_trps_part(tb, ch)

            def emit_ldkv(key, h, gi):
                bi = gi % 3
                d = H.setdefault(key, {})
                d["bi"] = bi
                wait(p1, "gpsimd", s_t, wk_free[bi])
                d["ldk"] = dma(p1, "gpsimd", s_wk[bi], wk_sl[:, bi], wk_img[h])
                wait(p1, "gpsimd", s_t, wv_free[bi])
                d["ldv"] = dma(p1, "gpsimd", s_wv[bi], wv_sl[:, bi], wv_img[h])

            def emit_kvproj(key, prev):
                tb, h = key
                hp = h % 2
                par = tb % 2
                d = H[key]
                bi = d["bi"]
                wait(p1, "tensor", s_wk[bi], d["ldk"])
                wait(p1, "tensor", s_v, XN_READY[par])
                if prev is not None:
                    wait(p1, "tensor", s_s, H[prev]["ksb"])
                for kc in range(NKC - 1):
                    op(p1, "tensor", lambda E, kc=kc, bi=bi, par=par: E.matmul(
                        k_ps[:], wk_sl[:, bi, kc, :], xn_tb[:, par, kc, :],
                        start=(kc == 0), stop=False))
                wk_free[bi] = inc(p1, "tensor", s_t, lambda E, bi=bi, par=par: E.matmul(
                    k_ps[:], wk_sl[:, bi, NKC - 1, :], xn_tb[:, par, NKC - 1, :],
                    start=False, stop=True))
                d["kps"] = s_t.n
                wait(p1, "scalar", s_t, d["kps"])
                d["ksb"] = inc(p1, "scalar", s_s, lambda E, h=h, hp=hp: E.activation(
                    k_sb[:, hp, :], k_ps[:], AF.Identity, bias=bk_sb[:, h:h + 1]))
                wait(p1, "tensor", s_wv[bi], d["ldv"])
                if prev is not None:
                    wait(p1, "tensor", s_s, H[prev]["vsb"])
                for kc in range(NKC - 1):
                    op(p1, "tensor", lambda E, kc=kc, bi=bi, par=par: E.matmul(
                        v_ps[:], wv_sl[:, bi, kc, :], xn_tb[:, par, kc, :],
                        start=(kc == 0), stop=False))
                wv_free[bi] = inc(p1, "tensor", s_t, lambda E, bi=bi, par=par: E.matmul(
                    v_ps[:], wv_sl[:, bi, NKC - 1, :], xn_tb[:, par, NKC - 1, :],
                    start=False, stop=True))
                d["vps"] = s_t.n
                wait(p1, "scalar", s_t, d["vps"])
                d["vsb"] = inc(p1, "scalar", s_s, lambda E, h=h, hp=hp: E.activation(
                    v_sb[:, hp, :], v_ps[:], AF.Identity, bias=bv_sb[:, h:h + 1]))

            def emit_k2elu(key, prev):
                tb, h = key
                hp = h % 2
                d = H[key]
                wait(p1, "tensor", s_s, d["ksb"])
                pv = max(H[prev]["k2sb"], H[prev]["vtm"], H[prev]["ksadd"]) if prev is not None else 0
                wait(p1, "tensor", s_v, pv)
                for sc4 in range(3):
                    op(p1, "tensor", lambda E, sc4=sc4, st=(sc4 == 0), hp=hp: E.matmul(
                        k2_ps[:, sc4 * 64:(sc4 + 1) * 64],
                        k_sb[:, hp, sc4 * 128:(sc4 + 1) * 128], wmk_bf[:],
                        start=st, stop=True, skip_group_check=True))
                g_k2 = inc(p1, "tensor", s_t, lambda E, hp=hp: E.matmul(
                    k2_ps[:, 3 * 64:4 * 64],
                    k_sb[:, hp, 3 * 128:4 * 128], wmk_bf[:],
                    start=False, stop=True, skip_group_check=True))
                wait(p1, "scalar", s_t, g_k2)
                gs1 = inc(p1, "scalar", s_s, lambda E: E.activation(et[:, 1, :], k2_ps[:], AF.Exp))
                wait(p1, "vector", s_t, g_k2)
                wait(p1, "vector", s_s, gs1)
                d["elu_rd"] = seq(p1, "vector", [lambda E: E.tensor_scalar(
                    et[:, 2, :], k2_ps[:], 0.0, None, ALU.max)])
                d["k2sb"] = inc(p1, "vector", s_v, lambda E, hp=hp: E.scalar_tensor_tensor(
                    k2_sb[:, hp, :, :].rearrange("p a b -> p (a b)"), et[:, 1, :], 1.0,
                    et[:, 2, :], ALU.min, ALU.add))

            def emit_vt(key):
                tb, h = key
                hp = h % 2
                d = H[key]
                wait(p1, "tensor", s_s, d["vsb"])
                wait(p1, "tensor", s_v, T["trps_free"])
                for j in range(3):
                    op(p1, "tensor", lambda E, j=j, hp=hp: E.transpose(
                        trps[:, j * 128:(j + 1) * 128], v_sb[:, hp, j * 128:(j + 1) * 128], ident[:]))
                g_vt = inc(p1, "tensor", s_t, lambda E, hp=hp: E.transpose(
                    trps[:, 3 * 128:4 * 128], v_sb[:, hp, 3 * 128:4 * 128], ident[:]))
                wait(p1, "vector", s_t, g_vt)
                d["vtm"] = inc(p1, "vector", s_v, lambda E, hp=hp: E.tensor_copy(
                    v_tm[:, hp, :, :].rearrange("p a b -> p (a b)"), trps[:, 0:512]))
                T["trps_free"] = d["vtm"]

            def emit_kv(key):
                tb, h = key
                hp = h % 2
                d = H[key]
                bank = h // 4
                wait(p1, "tensor", s_v, max(d["k2sb"], d["vtm"]))
                for sc4 in range(4):
                    op(p1, "tensor", lambda E, h=h, sc4=sc4, st=not kv_started[bank], hp=hp: E.matmul(
                        kv_ps[:, h, :], k2_sb[:, hp, sc4, :], v_tm[:, hp, sc4, :],
                        start=st, stop=True, skip_group_check=True))
                    kv_started[bank] = True
                    if sc4 < 3:
                        op(p1, "tensor", lambda E, sc4=sc4, st=(sc4 == 0), hp=hp: E.matmul(
                            kst[:], k2_sb[:, hp, sc4, :], ones_bf[:],
                            start=st, stop=True, skip_group_check=True))
                g_ks = inc(p1, "tensor", s_t, lambda E, hp=hp: E.matmul(
                    kst[:], k2_sb[:, hp, 3, :], ones_bf[:],
                    start=False, stop=True, skip_group_check=True))
                wait(p1, "vector", s_t, g_ks)
                d["ksadd"] = seq(p1, "vector", [lambda E, h=h: E.tensor_add(
                    ksum_acc[:, h:h + 1], ksum_acc[:, h:h + 1], kst[:])])

            seq(p1, "vector", [lambda E: E.memset(ksum_acc[:], 0.0)])
            for ch in range(4):
                emit_stageA_chunk(0, ch)
            XN_READY[0] = s_v.n

            own_done = set()
            heads = [(tb, h) for tb in range(ntb) for h in range(nh)]
            emit_ldkv(heads[0], heads[0][1], 0)
            if len(heads) > 1:
                emit_ldkv(heads[1], heads[1][1], 1)
            emit_kvproj(heads[0], None)
            for i, key in enumerate(heads):
                tb, h = key
                prev = heads[i - 1] if i > 0 else None
                full = (nh == NH)
                if full:
                    if tb + 1 < ntb and h in (6, 7, 9, 11):
                        emit_x_load(tb + 1, {6: 0, 7: 1, 9: 2, 11: 3}[h])
                    if tb in (1, 2) and h in (0, 2, 4, 6):
                        own_done.add((tb - 1) * 4 + h // 2)
                        emit_x_load(tb, (tb - 1) * 4 + h // 2, own=True)
                if i + 2 < len(heads):
                    emit_ldkv(heads[i + 2], heads[i + 2][1], i + 2)
                emit_k2elu(key, prev)
                emit_vt(key)
                if full:
                    if tb + 1 < ntb and h in (8, 10, 12, 14):
                        emit_ln_part(tb + 1, (h - 8) // 2)
                    if tb in (1, 2) and h in (1, 3, 5, 7):
                        emit_ln_part(tb, (tb - 1) * 4 + (h - 1) // 2, own=True)
                if i + 1 < len(heads):
                    emit_kvproj(heads[i + 1], key)
                emit_kv(key)
                if full:
                    if tb + 1 < ntb and h in (8, 10, 12, 14):
                        emit_trps_part(tb + 1, (h - 8) // 2)
                        if h == 14:
                            XN_READY[(tb + 1) % 2] = s_v.n
                    if tb in (1, 2) and h in (1, 3, 5, 7):
                        emit_trps_part(tb, (tb - 1) * 4 + (h - 1) // 2, own=True)
                if h == nh - 1:
                    XN_FREE[tb % 2] = H[key]["vps"]
                if nh < NH and h == nh - 1 and tb + 1 < ntb:
                    for ch in range(4):
                        emit_stageA_chunk(tb + 1, ch)
                    XN_READY[(tb + 1) % 2] = s_v.n
            for ch in range(NT):
                if ch not in own_done:
                    emit_x_load(0, ch, own=True)
                    emit_ln_part(0, ch, own=True)
                    emit_trps_part(0, ch, own=True)

            wait(p1, "vector", s_t, s_t.n)
            inc(p1, "vector", s_v, lambda E: E.tensor_copy(
                kv_bf[:, 0:nh, :].rearrange("p a b -> p (a b)"),
                kv_ps[:, 0:nh, :].rearrange("p a b -> p (a b)")))
            inc(p1, "vector", s_v, lambda E: E.tensor_copy(ksum_bf[:, 0:nh], ksum_acc[:, 0:nh]))
            if debug:
                wait(p1, "sync", s_v, s_v.n)
                dma(p1, "sync", s_dbg, dbg["kv"][:, 0:nh, :].rearrange("p a b -> p (a b)") if nh < NH else dbg["kv"].rearrange("p a b -> p (a b)"), kv_bf[:, 0:nh, :].rearrange("p a b -> p (a b)"))
                dma(p1, "sync", s_dbg, dbg["ksum"][:, 0:nh] if nh < NH else dbg["ksum"], ksum_bf[:, 0:nh])
                wait(p1, "sync", s_dbg)
            barrier(p1)
            run(p1)
        T["trps_free"] = 0
        T["xn_free"] = 0

        acc = sb("acc", [128, NT, HID], F32)
        if upto >= 2:
          with contextlib.ExitStack() as ph:
            lsb = lambda n, s, d: ph.enter_context(nc.sbuf_tensor(n, s, d))
            attn_fm = lsb("attn_fm", [128, NKC, 1024], BF16)

            if debug:
                pxa = Phase()
                dma(pxa, "sync", s_dbg, dbg["xn"], xn_own[:])
                wait(pxa, "sync", s_dbg)
                run(pxa)

            woq_h = {}
            wo_ld = [0, 0, 0, 0]
            for half in range(2):
                if half == 1:
                    woq_h["t"] = ph.enter_context(nc.sbuf_tensor("woq", [128, 2, NKC, 512], BF16))
                with contextlib.ExitStack() as hh:
                    q2h = hh.enter_context(nc.sbuf_tensor(f"q2h{half}", [64, NH, 512], BF16))
                    wq_sl = hh.enter_context(nc.sbuf_tensor(f"wq_sl{half}", [128, 3, NKC, 128], BF16))
                    qf_sb = hh.enter_context(nc.sbuf_tensor(f"qf_sb{half}", [128, 2, 512], BF16))
                    etq = hh.enter_context(nc.sbuf_tensor(f"etq{half}", [128, 2, 512], F32))
                    with (
                        nc.psum_tensor(f"q_ps{half}", [128, 512], F32) as q_ps,
                        nc.psum_tensor(f"q2_ps{half}", [64, 512], F32) as q2_ps,
                    ):
                        p2b = Phase()
                        phase_sync(p2b)
                        wq_free = [0, 0, 0]
                        QH = {}

                        def q_ldkv(h):
                            bi = h % 3
                            wait(p2b, "gpsimd", s_t, wq_free[bi])
                            QH.setdefault(h, {})["ldq"] = dma(
                                p2b, "gpsimd", s_wq[bi], wq_sl[:, bi], wq_img[h])

                        def q_proj(h):
                            bi = h % 3
                            d = QH[h]
                            wait(p2b, "tensor", s_wq[bi], d["ldq"])
                            if h >= 1:
                                wait(p2b, "tensor", s_s, QH[h - 1]["qf"])
                            for kc in range(NKC - 1):
                                op(p2b, "tensor", lambda E, kc=kc, bi=bi, half=half: E.matmul(
                                    q_ps[:], wq_sl[:, bi, kc, :],
                                    xn_own[:, kc, half * 512:(half + 1) * 512],
                                    start=(kc == 0), stop=False))
                            wq_free[bi] = inc(p2b, "tensor", s_t, lambda E, bi=bi, half=half: E.matmul(
                                q_ps[:], wq_sl[:, bi, NKC - 1, :],
                                xn_own[:, NKC - 1, half * 512:(half + 1) * 512],
                                start=False, stop=True))
                            wait(p2b, "scalar", s_t, s_t.n)
                            d["qf"] = inc(p2b, "scalar", s_s, lambda E, h=h, qb=h % 2: E.activation(
                                qf_sb[:, qb, :], q_ps[:], AF.Identity, bias=bq_sb[:, h:h + 1]))

                        def q_map(h):
                            bi = h % 2
                            d = QH[h]
                            wait(p2b, "tensor", s_s, d["qf"])
                            if h >= 1:
                                wait(p2b, "tensor", s_v, QH[h - 1]["q2sb"])
                            g_q2 = inc(p2b, "tensor", s_t, lambda E, qb=h % 2: E.matmul(
                                q2_ps[:], wmq_bf[:], qf_sb[:, qb, :], start=True, stop=True))
                            wait(p2b, "scalar", s_t, g_q2)
                            gs1 = inc(p2b, "scalar", s_s, lambda E: E.activation(
                                etq[0:64, 1, :], q2_ps[:], AF.Exp))
                            wait(p2b, "vector", s_t, g_q2)
                            wait(p2b, "vector", s_s, gs1)
                            seq(p2b, "vector", [lambda E: E.tensor_scalar(
                                etq[0:64, 0, :], q2_ps[:], 0.0, None, ALU.max)])
                            d["q2sb"] = inc(p2b, "vector", s_v, lambda E, h=h: E.scalar_tensor_tensor(
                                q2h[:, h, :], etq[0:64, 1, :], 1.0,
                                etq[0:64, 0, :], ALU.min, ALU.add))

                        q_ldkv(0)
                        q_ldkv(1)
                        q_proj(0)
                        for h in range(NH):
                            if h + 2 < NH:
                                q_ldkv(h + 2)
                            if h + 1 < NH:
                                q_proj(h + 1)
                            q_map(h)
                        barrier(p2b)
                        run(p2b)

                    with (
                        nc.psum_tensor(f"out_ps{half}", [128, HID], F32) as out_ps,
                        nc.psum_tensor(f"nm_ps{half}", [128, 2, 256], F32) as nm_ps,
                        nc.psum_tensor(f"trps3_{half}", [128, 1024], BF16) as trps,
                    ):
                        p2c = Phase()
                        phase_sync(p2c)
                        attn_tm = hh.enter_context(nc.sbuf_tensor(f"attn_tm{half}", [128, HID], BF16))
                        g_evac_prev = [0, 0]
                        g_rn_rd = [0, 0]
                        for t in range(4):
                            ti = half * 4 + t
                            tp = t % 2
                            wait(p2c, "tensor", s_v, g_rn_rd[tp])
                            for h in range(NH - 1):
                                op(p2c, "tensor", lambda E, h=h, t=t, tp=tp: E.matmul(
                                    nm_ps[:, tp, h:h + 1], q2h[:, h, t * 128:(t + 1) * 128],
                                    ksum_bf[:, h:h + 1], start=(h == 0), stop=True,
                                    skip_group_check=True))
                            g_nm = inc(p2c, "tensor", s_t, lambda E, t=t, tp=tp: E.matmul(
                                nm_ps[:, tp, NH - 1:NH], q2h[:, NH - 1, t * 128:(t + 1) * 128],
                                ksum_bf[:, NH - 1:NH], start=False, stop=True,
                                skip_group_check=True))
                            wait(p2c, "vector", s_t, g_nm)
                            g_rn = seq(p2c, "vector", [
                                (lambda tp: lambda E: E.tensor_scalar(
                                    rn_sc[:, 0:NH], nm_ps[:, tp, 0:NH], EPS_AT, None, ALU.add))(tp),
                                lambda E: E.reciprocal(rn_sc[:, 0:NH], rn_sc[:, 0:NH]),
                            ])
                            g_rn_rd[tp] = g_rn
                            wait(p2c, "tensor", s_v, g_evac_prev[0])
                            wait(p2c, "tensor", s_s, g_evac_prev[1])
                            for h in range(NH - 1):
                                op(p2c, "tensor", lambda E, h=h, t=t: E.matmul(
                                    out_ps[:, h * 128:(h + 1) * 128],
                                    q2h[:, h, t * 128:(t + 1) * 128], kv_bf[:, h, :],
                                    start=(h % 4 == 0), stop=True, skip_group_check=True))
                            g_out = inc(p2c, "tensor", s_t, lambda E, t=t: E.matmul(
                                out_ps[:, (NH - 1) * 128:NH * 128],
                                q2h[:, NH - 1, t * 128:(t + 1) * 128], kv_bf[:, NH - 1, :],
                                start=False, stop=True, skip_group_check=True))
                            wait(p2c, "vector", s_t, g_out)
                            wait(p2c, "vector", s_v, g_rn)
                            for h in range(NH - 1):
                                op(p2c, "vector", lambda E, h=h: E.tensor_scalar(
                                    attn_tm[:, h * 128:(h + 1) * 128],
                                    out_ps[:, h * 128:(h + 1) * 128],
                                    rn_sc[:, h:h + 1], None, ALU.mult))
                            g_ev_v = inc(p2c, "vector", s_v, lambda E: E.tensor_scalar(
                                attn_tm[:, (NH - 1) * 128:NH * 128],
                                out_ps[:, (NH - 1) * 128:NH * 128],
                                rn_sc[:, NH - 1:NH], None, ALU.mult))
                            g_evac_prev = [g_ev_v, 0]
                            wait(p2c, "tensor", s_v, g_ev_v)
                            emit_t16(p2c, trps, attn_tm,
                                     lambda batch, B, ti=ti: attn_fm[:, batch * B:(batch + 1) * B, ti * 128:(ti + 1) * 128])
                        if half == 1:
                            wo_ld[0] = dma(p2c, "gpsimd", s_wo2[0], woq_h["t"][:, 0], wo_img[0])
                            wo_ld[1] = dma(p2c, "gpsimd", s_wo2[1], woq_h["t"][:, 1], wo_img[1])
                        barrier(p2c)
                        run(p2c)
                    T["trps_free"] = 0

            if debug:
                pdbg = Phase()
                wait(pdbg, "sync", s_v, s_v.n)
                dma(pdbg, "sync", s_dbg, dbg["attn"], attn_fm[:])
                wait(pdbg, "sync", s_dbg)
                run(pdbg)

            with contextlib.ExitStack() as hh:
                woq = woq_h["t"]
                lg_sb = hh.enter_context(nc.sbuf_tensor("lg_sb", [128, NE], F32))
                rwork = hh.enter_context(nc.sbuf_tensor("rwork", [128, NE], F32))
                rsel = hh.enter_context(nc.sbuf_tensor("rsel", [128, NE], F32))
                rtmp = hh.enter_context(nc.sbuf_tensor("rtmp", [128, NE], F32))
                rex = hh.enter_context(nc.sbuf_tensor("rex", [128, NE], F32))
                with (
                    nc.psum_tensor("wo_ps", [128, 2, 512], F32) as wo_ps,
                    nc.psum_tensor("lg_ps", [128, 512], F32) as lg_ps,
                ):
                    p2d = Phase()
                    phase_sync(p2d)
                    lg_free = [0, 0]

                    def routing_tile(t):
                        for kc in range(NKC):
                            op(p2d, "tensor", lambda E, kc=kc, t=t: E.matmul(
                                lg_ps[:, 0:NE], xn_own[:, kc, t * 128:(t + 1) * 128],
                                gateT_bf[:, kc, :], start=(kc == 0), stop=(kc == NKC - 1)))
                        for kc in range(NKC - 1):
                            op(p2d, "tensor", lambda E, kc=kc, t=t: E.matmul(
                                lg_ps[:, 64:65], xn_own[:, kc, t * 128:(t + 1) * 128],
                                shg1_bf[:, kc, :], start=False, stop=False,
                                skip_group_check=True))
                        g_lg = inc(p2d, "tensor", s_t, lambda E, t=t: E.matmul(
                            lg_ps[:, 64:65], xn_own[:, NKC - 1, t * 128:(t + 1) * 128],
                            shg1_bf[:, NKC - 1, :], start=False, stop=True,
                            skip_group_check=True))
                        wait(p2d, "vector", s_t, g_lg)
                        g_cp = inc(p2d, "vector", s_v, lambda E: E.tensor_copy(lg_sb[:], lg_ps[:, 0:NE]))
                        wait(p2d, "scalar", s_t, g_lg)
                        g_sg = inc(p2d, "scalar", s_s, lambda E, t=t: E.activation(
                            shgate_sb[:, t:t + 1], lg_ps[:, 64:65], AF.Sigmoid))
                        lg_free[0] = g_cp
                        lg_free[1] = g_sg
                        wait(p2d, "vector", s_v, g_cp)
                        builders = [
                            lambda E: E.tensor_copy(rwork[:], lg_sb[:]),
                            lambda E: E.memset(rsel[:], 0.0),
                            lambda E: E.reduce_max(rt_sc[:, 1:2], lg_sb[:], axis=AX, negate=True),
                        ]
                        for j in range(TOPK):
                            builders += [
                                lambda E: E.reduce_max(rt_sc[:, 0:1], rwork[:], axis=AX),
                                lambda E: E.tensor_scalar(rtmp[:], rwork[:], rt_sc[:, 0:1], None, ALU.is_equal),
                                lambda E: E.tensor_scalar(rtmp[:], rtmp[:], -BIG, BIG, ALU.mult, ALU.add),
                                lambda E: E.tensor_add(rtmp[:], rtmp[:], iota_sb[:]),
                                lambda E: E.tensor_reduce(rt_sc[:, 2:3], rtmp[:], AX, ALU.min),
                                lambda E: E.tensor_scalar(rtmp[:], iota_sb[:], rt_sc[:, 2:3], None, ALU.is_equal),
                                lambda E: E.tensor_add(rsel[:], rsel[:], rtmp[:]),
                                lambda E: E.scalar_tensor_tensor(rwork[:], rtmp[:], -BIG, rwork[:], ALU.mult, ALU.add),
                            ]
                        g_tk = seq(p2d, "vector", builders)
                        wait(p2d, "scalar", s_v, g_tk)
                        g_ex = inc(p2d, "scalar", s_s, lambda E: E.activation(
                            rex[:], lg_sb[:], AF.Exp, bias=rt_sc[:, 1:2]))
                        wait(p2d, "vector", s_s, g_ex)
                        seq(p2d, "vector", [
                            lambda E: E.tensor_mul(rex[:], rex[:], rsel[:]),
                            lambda E: E.reduce_sum(rt_sc[:, 3:4], rex[:], axis=AX),
                            lambda E: E.reciprocal(rt_sc[:, 4:5], rt_sc[:, 3:4]),
                            (lambda t: lambda E: E.tensor_scalar(
                                W_sb[:, t, :], rex[:], rt_sc[:, 4:5], None, ALU.mult))(t),
                        ])

                    wo_free = [0, 0]
                    ps_free = [0, 0]
                    rt_next = [0]

                    def maybe_routing():
                        if upto >= 3 and rt_next[0] < NT:
                            wait(p2d, "tensor", s_v, lg_free[0])
                            wait(p2d, "tensor", s_s, lg_free[1])
                            routing_tile(rt_next[0])
                            rt_next[0] += 1

                    for qt in range(4):
                        bi = qt % 2
                        if qt >= 2:
                            wait(p2d, "gpsimd", s_t, wo_free[bi])
                            wo_ld[qt] = dma(p2d, "gpsimd", s_wo2[bi], woq[:, bi], wo_img[qt])
                        wait(p2d, "tensor", s_wo2[bi], wo_ld[qt])
                        for t in range(NT):
                            if qt > 0 and t == 4:
                                maybe_routing()
                            pp = (qt * NT + t) % 2
                            wait(p2d, "tensor", s_v, ps_free[pp])
                            for kc in range(NKC - 1):
                                op(p2d, "tensor", lambda E, kc=kc, t=t, bi=bi, pp=pp: E.matmul(
                                    wo_ps[:, pp, :], attn_fm[:, kc, t * 128:(t + 1) * 128],
                                    woq[:, bi, kc, :], start=(kc == 0), stop=False))
                            g_mm = inc(p2d, "tensor", s_t, lambda E, t=t, bi=bi, pp=pp: E.matmul(
                                wo_ps[:, pp, :], attn_fm[:, NKC - 1, t * 128:(t + 1) * 128],
                                woq[:, bi, NKC - 1, :], start=False, stop=True))
                            if t == NT - 1:
                                wo_free[bi] = g_mm
                            wait(p2d, "vector", s_t, g_mm)
                            ps_free[pp] = inc(p2d, "vector", s_v, lambda E, t=t, qt=qt, pp=pp: E.tensor_copy(
                                acc[:, t, qt * 512:(qt + 1) * 512], wo_ps[:, pp, :]))
                        maybe_routing()
                    for _ in range(NT):
                        maybe_routing()
                    if debug:
                        wait(p2d, "sync", s_v, s_v.n)
                        wait(p2d, "sync", s_s, s_s.n)
                        dma(p2d, "sync", s_dbg, dbg["accw"], acc[:])
                        dma(p2d, "sync", s_dbg, dbg["W"], W_sb[:])
                        dma(p2d, "sync", s_dbg, dbg["shg"], shgate_sb[:])
                        wait(p2d, "sync", s_dbg)
                    barrier(p2d)
                    run(p2d)


        sh_state = {}
        if upto >= 4:
          sh_outer = contextlib.ExitStack()
          est.enter_context(sh_outer)
          shg_bf = sh_outer.enter_context(nc.sbuf_tensor("shg_bf", [128, NKC, NI], BF16))
          shu_bf = sh_outer.enter_context(nc.sbuf_tensor("shu_bf", [128, NKC, NI], BF16))
          with contextlib.ExitStack() as ph:
            lsb = lambda n, s, d: ph.enter_context(nc.sbuf_tensor(n, s, d))
            gu_sl = lsb("gu_sl", [128, 2, NKC, 256], BF16)
            dp_sl = lsb("dp_sl", [128, 2, 4, HID], BF16)
            inter = lsb("inter", [128, 4, 1024], BF16)
            sil_t = lsb("sil_t", [128, 1024], F32)
            with (
                nc.psum_tensor("gu_ps", [128, 2, 2, 512], F32) as gu_ps,
                nc.psum_tensor("dn_ps", [128, 2, 1024], F32) as dn_ps,
            ):
                p3b = Phase()
                phase_sync(p3b)
                gu_free = [0, 0]
                dp_free = [0, 0]
                dn_free = [0, 0]
                slot_sil = [0, 0]
                slot_int = [0, 0]
                g_int_prev = [0, 0, 0, 0]
                for e in range(ne):
                    dbi = e % 2
                    if e == max(ne - 6, 0):
                        sh_state["l1"] = dma(p3b, "gpsimd", s_w, shg_bf[:], shg_img)
                        sh_state["l2"] = dma(p3b, "gpsimd", s_w, shu_bf[:], shu_img)
                    wait(p3b, "gpsimd", s_t, dp_free[dbi])
                    ldd = dma(p3b, "gpsimd", s_dp[dbi], dp_sl[:, dbi], dp_img[e])
                    for k in range(4):
                        qbi = (e * 4 + k) % 2
                        wait(p3b, "gpsimd", s_t, gu_free[qbi])
                        ldg = dma(p3b, "gpsimd", s_gu[qbi], gu_sl[:, qbi], gu_img[e, k])
                        wait(p3b, "tensor", s_gu[qbi], ldg)
                        wait(p3b, "tensor", s_v, g_int_prev[k])
                        g_int = 0
                        for th in range(2):
                            slot = (k * 2 + th) % 2
                            wait(p3b, "tensor", s_s, slot_sil[slot])
                            wait(p3b, "tensor", s_v, slot_int[slot])
                            for kc in range(NKC):
                                op(p3b, "tensor", lambda E, kc=kc, th=th, qbi=qbi, slot=slot: E.matmul(
                                    gu_ps[:, slot, 0, :],
                                    gu_sl[:, qbi, kc, 0:128],
                                    xn_own[:, kc, th * 512:(th + 1) * 512],
                                    start=(kc == 0), stop=(kc == NKC - 1)))
                                if not (kc == NKC - 1):
                                    op(p3b, "tensor", lambda E, kc=kc, th=th, qbi=qbi, slot=slot: E.matmul(
                                        gu_ps[:, slot, 1, :],
                                        gu_sl[:, qbi, kc, 128:256],
                                        xn_own[:, kc, th * 512:(th + 1) * 512],
                                        start=(kc == 0), stop=(kc == NKC - 1)))
                            g_last = inc(p3b, "tensor", s_t, lambda E, th=th, qbi=qbi, slot=slot: E.matmul(
                                gu_ps[:, slot, 1, :],
                                gu_sl[:, qbi, NKC - 1, 128:256],
                                xn_own[:, NKC - 1, th * 512:(th + 1) * 512],
                                start=False, stop=True))
                            if th == 1:
                                gu_free[qbi] = g_last
                            wait(p3b, "scalar", s_t, g_last)
                            slot_sil[slot] = inc(p3b, "scalar", s_s, lambda E, slot=slot, th=th: E.activation(
                                sil_t[:, th * 512:(th + 1) * 512], gu_ps[:, slot, 0, :], AF.Silu))
                            wait(p3b, "vector", s_s, slot_sil[slot])
                            g_int = inc(p3b, "vector", s_v, lambda E, k=k, th=th, slot=slot: E.tensor_mul(
                                inter[:, k, th * 512:(th + 1) * 512],
                                sil_t[:, th * 512:(th + 1) * 512], gu_ps[:, slot, 1, :]))
                            slot_int[slot] = g_int
                        g_int_prev[k] = g_int
                    wait(p3b, "tensor", s_dp[dbi], ldd)
                    wait(p3b, "tensor", s_v, g_int_prev[3])
                    g_dn_last = 0
                    for t in range(NT):
                        for oh in range(2):
                            pp = (t * 2 + oh) % 2
                            wait(p3b, "tensor", s_v, dn_free[pp])
                            for k in range(4):
                                for o2 in range(2):
                                    if not (k == 3 and o2 == 1):
                                        op(p3b, "tensor", lambda E, k=k, o2=o2, oh=oh, t=t, dbi=dbi, pp=pp: E.matmul(
                                            dn_ps[:, pp, o2 * 512:(o2 + 1) * 512],
                                            inter[:, k, t * 128:(t + 1) * 128],
                                            dp_sl[:, dbi, k, oh * 1024 + o2 * 512: oh * 1024 + (o2 + 1) * 512],
                                            start=(k == 0), stop=(k == 3)))
                            g_dn = inc(p3b, "tensor", s_t, lambda E, oh=oh, t=t, dbi=dbi, pp=pp: E.matmul(
                                dn_ps[:, pp, 512:1024],
                                inter[:, 3, t * 128:(t + 1) * 128],
                                dp_sl[:, dbi, 3, oh * 1024 + 512: oh * 1024 + 1024],
                                start=False, stop=True))
                            g_dn_last = g_dn
                            wait(p3b, "vector", s_t, g_dn)
                            wait(p3b, "vector", s_v)
                            dn_free[pp] = inc(p3b, "vector", s_v, lambda E, t=t, oh=oh, e=e, pp=pp: E.scalar_tensor_tensor(
                                acc[:, t, oh * 1024:(oh + 1) * 1024], dn_ps[:, pp, :],
                                W_sb[:, t, e:e + 1], acc[:, t, oh * 1024:(oh + 1) * 1024],
                                ALU.mult, ALU.add))
                    dp_free[dbi] = g_dn_last
                barrier(p3b)
                run(p3b)

        if upto >= 5:
          with contextlib.ExitStack() as ph:
            lsb = lambda n, s, d: ph.enter_context(nc.sbuf_tensor(n, s, d))
            shd_bf = lsb("shd_bf", [128, 4, HID], BF16)
            inter = lsb("inter_sh", [128, 4, 1024], BF16)
            sil_t = lsb("sil_t2", [128, 1024], F32)
            x_t2f = lsb("x_t2c", [128, 2, HID], F32)
            o_t2 = lsb("o_t2", [128, 2, HID], F32)
            bo_b = lsb("bo_b", [128, HID], F32)
            with (
                nc.psum_tensor("sgu_ps", [128, 2, 1024], F32) as gu_ps,
                nc.psum_tensor("sdn_ps", [128, 2, 1024], F32) as dn_ps,
            ):
                p3c = Phase()
                phase_sync(p3c)
                ldb = dma(p3c, "sync", s_x, bo_b[:], bo_bc)
                wait(p3c, "vector", s_x, ldb)
                F4 = dict(xf=[0, 0], of=[0, 0], xld={})

                def fin_load(t):
                    bi = t % 2
                    wait(p3c, "sync", s_v, F4["xf"][bi])
                    F4["xld"][t] = dma(p3c, "sync", s_xp[bi], x_t2f[:, bi, :],
                                       x_own[t * 128:(t + 1) * 128, :])

                def fin_tile(t):
                    bi = t % 2
                    wait(p3c, "vector", s_xp[bi], F4["xld"].pop(t))
                    wait(p3c, "vector", s_op[bi], F4["of"][bi])
                    wait(p3c, "vector", s_v)
                    F4["xf"][bi] = seq(p3c, "vector", [
                        (lambda t, bi: lambda E: E.tensor_add(
                            o_t2[:, bi, :], acc[:, t, :], x_t2f[:, bi, :]))(t, bi),
                        (lambda bi: lambda E: E.tensor_add(
                            o_t2[:, bi, :], o_t2[:, bi, :], bo_b[:]))(bi),
                    ])
                    wait(p3c, "sync", s_v, F4["xf"][bi])
                    F4["of"][bi] = dma(p3c, "sync", s_op[bi],
                                       out_t[t * 128:(t + 1) * 128, :], o_t2[:, bi, :])
                sh_state["l3"] = dma(p3c, "gpsimd", s_wo2[0], shd_bf[:], shd_img)
                wait(p3c, "tensor", s_w, sh_state["l2"])
                g_int_prev = 0
                for k in range(4):
                    wait(p3c, "tensor", s_v, g_int_prev)
                    for kc in range(NKC):
                        for th in range(2):
                            op(p3c, "tensor", lambda E, kc=kc, th=th, k=k: E.matmul(
                                gu_ps[:, 0, th * 512:(th + 1) * 512],
                                shg_bf[:, kc, k * 128:(k + 1) * 128],
                                xn_own[:, kc, th * 512:(th + 1) * 512],
                                start=(kc == 0), stop=(kc == NKC - 1)))
                            if not (kc == NKC - 1 and th == 1):
                                op(p3c, "tensor", lambda E, kc=kc, th=th, k=k: E.matmul(
                                    gu_ps[:, 1, th * 512:(th + 1) * 512],
                                    shu_bf[:, kc, k * 128:(k + 1) * 128],
                                    xn_own[:, kc, th * 512:(th + 1) * 512],
                                    start=(kc == 0), stop=(kc == NKC - 1)))
                    g_gu = inc(p3c, "tensor", s_t, lambda E, k=k: E.matmul(
                        gu_ps[:, 1, 512:1024],
                        shu_bf[:, NKC - 1, k * 128:(k + 1) * 128],
                        xn_own[:, NKC - 1, 512:1024],
                        start=False, stop=True))
                    wait(p3c, "scalar", s_t, g_gu)
                    g_sil = inc(p3c, "scalar", s_s, lambda E: E.activation(
                        sil_t[:], gu_ps[:, 0, :], AF.Silu))
                    wait(p3c, "vector", s_s, g_sil)
                    g_int = inc(p3c, "vector", s_v, lambda E, k=k: E.tensor_mul(
                        inter[:, k, :], sil_t[:], gu_ps[:, 1, :]))
                    g_int_prev = g_int
                    wait(p3c, "tensor", s_s, g_sil)
                    wait(p3c, "tensor", s_v, g_int)
                dn_free = [0, 0]
                wait(p3c, "tensor", s_wo2[0], sh_state["l3"])
                fin_load(0)
                fin_load(1)
                for t in range(NT):
                    for oh in range(2):
                        pp = (t * 2 + oh) % 2
                        wait(p3c, "tensor", s_v, dn_free[pp])
                        for k in range(4):
                            for o2 in range(2):
                                if not (k == 3 and o2 == 1):
                                    op(p3c, "tensor", lambda E, k=k, o2=o2, oh=oh, t=t, pp=pp: E.matmul(
                                        dn_ps[:, pp, o2 * 512:(o2 + 1) * 512],
                                        inter[:, k, t * 128:(t + 1) * 128],
                                        shd_bf[:, k, oh * 1024 + o2 * 512: oh * 1024 + (o2 + 1) * 512],
                                        start=(k == 0), stop=(k == 3)))
                        g_dn = inc(p3c, "tensor", s_t, lambda E, oh=oh, t=t, pp=pp: E.matmul(
                            dn_ps[:, pp, 512:1024],
                            inter[:, 3, t * 128:(t + 1) * 128],
                            shd_bf[:, 3, oh * 1024 + 512: oh * 1024 + 1024],
                            start=False, stop=True))
                        wait(p3c, "vector", s_t, g_dn)
                        wait(p3c, "vector", s_v)
                        dn_free[pp] = inc(p3c, "vector", s_v, lambda E, t=t, oh=oh, pp=pp: E.scalar_tensor_tensor(
                            acc[:, t, oh * 1024:(oh + 1) * 1024], dn_ps[:, pp, :],
                            shgate_sb[:, t:t + 1], acc[:, t, oh * 1024:(oh + 1) * 1024],
                            ALU.mult, ALU.add))
                    fin_tile(t)
                    if t + 2 < NT:
                        fin_load(t + 2)
                wait(p3c, "sync", s_op[0])
                wait(p3c, "sync", s_op[1])
                wait(p3c, "vector", s_op[0])
                wait(p3c, "vector", s_op[1])
                barrier(p3c)
                run(p3c)

        if upto < 5:
          with contextlib.ExitStack() as ph:
            x_t2 = ph.enter_context(nc.sbuf_tensor("x_t2c", [128, 2, HID], F32))
            o_t2 = ph.enter_context(nc.sbuf_tensor("o_t2", [128, 2, HID], F32))
            bo_b = ph.enter_context(nc.sbuf_tensor("bo_b", [128, HID], F32))
            p4 = Phase()
            phase_sync(p4)
            seq(p4, "vector", [lambda E: E.memset(acc[:].rearrange("p a b -> p (a b)"), 0.0)])
            ldb = dma(p4, "sync", s_x, bo_b[:], bo_bc)
            wait(p4, "vector", s_x, ldb)
            xf = [0, 0]
            of = [0, 0]
            for t in range(NT):
                bi = t % 2
                wait(p4, "sync", s_v, xf[bi])
                ld = dma(p4, "sync", s_xp[bi], x_t2[:, bi, :], x_own[t * 128:(t + 1) * 128, :])
                wait(p4, "vector", s_xp[bi], ld)
                wait(p4, "vector", s_op[bi], of[bi])
                xf[bi] = seq(p4, "vector", [
                    (lambda t, bi: lambda E: E.tensor_add(
                        o_t2[:, bi, :], acc[:, t, :], x_t2[:, bi, :]))(t, bi),
                    (lambda bi: lambda E: E.tensor_add(
                        o_t2[:, bi, :], o_t2[:, bi, :], bo_b[:]))(bi),
                ])
                wait(p4, "sync", s_v, xf[bi])
                of[bi] = dma(p4, "sync", s_op[bi], out_t[t * 128:(t + 1) * 128, :], o_t2[:, bi, :])
            wait(p4, "sync", s_op[0])
            wait(p4, "sync", s_op[1])
            wait(p4, "vector", s_op[0])
            wait(p4, "vector", s_op[1])
            run(p4)

    return nc


def prep_shared(ins):
    C = np.ascontiguousarray
    f = np.float32
    w = {}
    wqT = ins["L0_attn_wq"].T
    wkT = ins["L0_attn_wk"].T
    wvT = ins["L0_attn_wv"].T
    woT = ins["L0_attn_wo"].T
    img = lambda m: C(np.asarray(m, f).reshape(NKC, 128, NH, 128).transpose(2, 1, 0, 3))
    w["wq_img"] = img(wqT)
    w["wk_img"] = img(wkT)
    w["wv_img"] = img(wvT)
    w["wo_img"] = C(np.asarray(woT, f).reshape(NKC, 128, 4, 512).transpose(2, 1, 0, 3))
    w["bq_r"] = C(np.asarray(ins["L0_attn_bq"], f).reshape(NH, 128).T)
    w["bk_r"] = C(np.asarray(ins["L0_attn_bk"], f).reshape(NH, 128).T)
    w["bv_r"] = C(np.asarray(ins["L0_attn_bv"], f).reshape(NH, 128).T)
    w["bo_bc"] = C(np.broadcast_to(np.asarray(ins["L0_attn_bo"], f), (128, HID)))
    w["wmap_qT"] = C(np.asarray(ins["L0_attn_wmap_q"], f).T)
    w["wmap_kT"] = C(np.asarray(ins["L0_attn_wmap_k"], f).T)
    w["gate_img"] = C(np.asarray(ins["L0_ffn_gate_w"], f).T.reshape(NKC, 128, NE).transpose(1, 0, 2))
    guT = np.asarray(ins["L0_ffn_gu_w"], f).transpose(0, 2, 1)
    gu4 = guT.reshape(NE, HID, 8, 128)[:, :, [0, 4, 1, 5, 2, 6, 3, 7], :]
    gu4 = gu4.reshape(NE, NKC, 128, 4, 256).transpose(0, 3, 2, 1, 4)
    w["gu_img"] = C(gu4)
    dpT = np.asarray(ins["L0_ffn_dp_w"], f).transpose(0, 2, 1)
    w["dp_img"] = C(dpT.reshape(NE, 4, 128, HID).transpose(0, 2, 1, 3))
    shim = lambda m: C(np.asarray(m, f).T.reshape(NKC, 128, NI).transpose(1, 0, 2))
    w["shg_img"] = shim(ins["L0_ffn_sh_gate_w"])
    w["shu_img"] = shim(ins["L0_ffn_sh_up_w"])
    w["shd_img"] = C(np.asarray(ins["L0_ffn_sh_down_w"], f).T.reshape(4, 128, HID).transpose(1, 0, 2))
    w["shg1_img"] = C(np.asarray(ins["L0_ffn_sh_gate1_w"], f).T.reshape(NKC, 128, 1).transpose(1, 0, 2))
    w["ident_bf"] = np.eye(128, dtype=ml_dtypes.bfloat16)
    w["ident_f"] = np.eye(128, dtype=np.float32)
    return w


def make_in_maps(ins):
    shared = prep_shared(ins)
    x = np.asarray(ins["L0_attn_hidden_states"], np.float32)
    maps = []
    for c in range(8):
        b, qd = c // 4, c % 4
        m = dict(shared)
        m["x_full"] = np.ascontiguousarray(x[b])
        m["x_own"] = np.ascontiguousarray(x[b, qd * 1024:(qd + 1) * 1024])
        maps.append(m)
    return maps


def kernel(**inputs):
    maps = make_in_maps(inputs)
    nc = build_nc(debug=False)
    res = run_bass_kernel_spmd(nc, maps, core_ids=list(range(8)))
    out = np.concatenate([res.results[c]["out"] for c in range(8)], axis=0)
    return out.reshape(2, 4096, 2048)


# version 14
# speedup vs baseline: 1.0102x; 1.0102x over previous
import contextlib
import numpy as np
import ml_dtypes
import concourse.bass as bass
import concourse.mybir as mybir
from concourse.bass_utils import run_bass_kernel_spmd

F32, BF16 = mybir.dt.float32, mybir.dt.bfloat16
AF, ALU, AX = mybir.ActivationFunctionType, mybir.AluOpType, mybir.AxisListType.X

HID, S, NH, DIM, NE, NI = 2048, 4096, 16, 64, 64, 512
TOPK = 8
EPS_LN, EPS_AT = 1e-5, 1e-6
BIG = 1e30
NKC = HID // 128
NT = 8
NTB = 8

ENGS = ("sync", "gpsimd", "vector", "scalar", "tensor")


class Sem:
    __slots__ = ("h", "n")
    def __init__(self, h):
        self.h, self.n = h, 0


class Phase:
    def __init__(self):
        self.q = {e: [] for e in ENGS}

    def emit(self, eng, f):
        self.q[eng].append(f)

    def run(self, block):
        for name in ENGS:
            fns = self.q[name]
            if fns:
                getattr(block, name)(lambda E, fns=fns: [f(E) for f in fns])


def build_nc(debug=False, upto=9, ntb=NTB, nh=NH, ne=NE):
    nc = bass.Bass("TRN2", target_bir_lowering=False, debug=False)
    dt = lambda n, s, k="ExternalInput", d=F32: nc.dram_tensor(n, s, d, kind=k)

    x_full = dt("x_full", [S, HID]).ap()
    x_own = dt("x_own", [1024, HID]).ap()
    wq_img = dt("wq_img", [NH, 128, NKC, 128]).ap()
    wk_img = dt("wk_img", [NH, 128, NKC, 128]).ap()
    wv_img = dt("wv_img", [NH, 128, NKC, 128]).ap()
    wo_img = dt("wo_img", [4, 128, NKC, 512]).ap()
    bq_r = dt("bq_r", [128, NH]).ap()
    bk_r = dt("bk_r", [128, NH]).ap()
    bv_r = dt("bv_r", [128, NH]).ap()
    bo_bc = dt("bo_bc", [128, HID]).ap()
    wmap_q = dt("wmap_qT", [128, DIM]).ap()
    wmap_k = dt("wmap_kT", [128, DIM]).ap()
    gate_img = dt("gate_img", [128, NKC, NE]).ap()
    gu_img = dt("gu_img", [NE, 4, 128, NKC, 256]).ap()
    dp_img = dt("dp_img", [NE, 128, 4, HID]).ap()
    shg_img = dt("shg_img", [128, NKC, NI]).ap()
    shu_img = dt("shu_img", [128, NKC, NI]).ap()
    shd_img = dt("shd_img", [128, 4, HID]).ap()
    shg1_img = dt("shg1_img", [128, NKC, 1]).ap()
    ident_in = dt("ident_bf", [128, 128], d=BF16).ap()
    identf_in = dt("ident_f", [128, 128]).ap()
    out_t = dt("out", [1024, HID], k="ExternalOutput").ap()
    dbg = {}
    if debug:
        dbg["xn"] = dt("d_xn", [128, NKC, 1024], k="ExternalOutput", d=BF16).ap()
        dbg["kv"] = dt("d_kv", [64, NH, 128], k="ExternalOutput", d=BF16).ap()
        dbg["ksum"] = dt("d_ksum", [64, NH], k="ExternalOutput", d=BF16).ap()
        dbg["attn"] = dt("d_attn", [128, NKC, 1024], k="ExternalOutput", d=BF16).ap()
        dbg["accw"] = dt("d_accw", [128, NT, HID], k="ExternalOutput").ap()
        dbg["W"] = dt("d_W", [128, NT, NE], k="ExternalOutput").ap()
        dbg["shg"] = dt("d_shg", [128, NT], k="ExternalOutput").ap()

    est = contextlib.ExitStack()
    with est:
        sb = lambda n, s, d: est.enter_context(nc.sbuf_tensor(n, s, d))
        mksem = lambda n: Sem(est.enter_context(nc.semaphore(n)))

        xn_own = sb("xn_own", [128, NKC, 1024], BF16)
        kv_bf = sb("kv_bf", [64, NH, 128], BF16)
        ksum_bf = sb("ksum_bf", [64, NH], BF16)
        W_sb = sb("W_sb", [128, NT, NE], F32)
        shgate_sb = sb("shgate_sb", [128, NT], F32)
        gateT_bf = sb("gateT_bf", [128, NKC, NE], BF16)
        shg1_bf = sb("shg1_bf", [128, NKC, 1], BF16)
        wmq_bf = sb("wmq_bf", [128, DIM], BF16)
        wmk_bf = sb("wmk_bf", [128, DIM], BF16)
        bq_sb = sb("bq_sb", [128, NH], F32)
        bk_sb = sb("bk_sb", [128, NH], F32)
        bv_sb = sb("bv_sb", [128, NH], F32)
        ident = sb("ident", [128, 128], BF16)
        ones_bf = sb("ones_bf", [128, 1], BF16)
        iota_sb = sb("iota_sb", [128, NE], F32)
        lnsc = sb("lnsc", [128, 8], F32)
        rt_sc = sb("rt_sc", [128, 8], F32)
        rn_sc = sb("rn_sc", [128, NH + 2], F32)
        rn2_sc = sb("rn2_sc", [128, NH + 2], F32)

        s_x = mksem("s_x")
        s_w = mksem("s_w")
        s_xp = [mksem("s_xp0"), mksem("s_xp1")]
        s_wk = [mksem("s_wk0"), mksem("s_wk1"), mksem("s_wk2")]
        s_wv = [mksem("s_wv0"), mksem("s_wv1"), mksem("s_wv2")]
        s_wq = [mksem("s_wq0"), mksem("s_wq1"), mksem("s_wq2")]
        s_wo2 = [mksem("s_wo0"), mksem("s_wo1")]
        s_gu = [mksem("s_gu0"), mksem("s_gu1")]
        s_dp = [mksem("s_dp0"), mksem("s_dp1")]
        s_op = [mksem("s_op0"), mksem("s_op1")]
        s_dbg = mksem("s_dbg")
        s_t = mksem("s_t")
        s_v = mksem("s_v")
        s_s = mksem("s_s")
        s_o = mksem("s_o")
        s_ph = mksem("s_ph")

        block = est.enter_context(nc.Block())

        def dma(p, eng, sem, out, in_):
            p.emit(eng, lambda E, out=out, in_=in_: E.dma_start(out=out, in_=in_).then_inc(sem.h, 16))
            sem.n += 16
            return sem.n

        def wait(p, eng, sem, val=None):
            v = sem.n if val is None else val
            if v > 0:
                p.emit(eng, lambda E, v=v: E.wait_ge(sem.h, v))

        def inc(p, eng, sem, f):
            p.emit(eng, lambda E, f=f: f(E).then_inc(sem.h, 1))
            sem.n += 1
            return sem.n

        def op(p, eng, f):
            p.emit(eng, f)

        def barrier(p):
            for e in ENGS:
                p.emit(e, lambda E: E.nop().then_inc(s_ph.h, 1))
            s_ph.n += len(ENGS)
            for e in ENGS:
                wait(p, e, s_ph)

        def seq(p, eng, builders):
            sem = {"vector": s_v, "scalar": s_s, "tensor": s_t}[eng]
            last = 0
            for b in builders:
                last = inc(p, eng, sem, b)
                wait(p, eng, sem, last)
            return last

        def phase_sync(p):
            for e in ("tensor", "vector", "scalar"):
                for sem in (s_v, s_s, s_t):
                    wait(p, e, sem)

        def run(p):
            p.run(block)

        T = dict(trps_free=0, xn_free=0)

        def emit_ln(p, x_tile, xn_bf_out):
            ln_xsq = T["xsq"]
            ventry = s_v.n
            seq(p, "vector", [
                lambda E: E.reduce_sum(lnsc[:, 0:1], x_tile, axis=AX),
                lambda E: E.tensor_scalar(lnsc[:, 1:2], lnsc[:, 0:1], 1.0 / HID, None, ALU.mult),
            ])
            wait(p, "scalar", s_v, ventry)
            g_sq = inc(p, "scalar", s_s, lambda E: E.activation(ln_xsq[:], x_tile, AF.Square))
            wait(p, "vector", s_s, g_sq)
            g_rv = seq(p, "vector", [
                lambda E: E.reduce_sum(lnsc[:, 2:3], ln_xsq[:], axis=AX),
                lambda E: E.tensor_scalar(lnsc[:, 3:4], lnsc[:, 2:3], 1.0 / HID, None, ALU.mult),
                lambda E: E.tensor_mul(lnsc[:, 4:5], lnsc[:, 1:2], lnsc[:, 1:2]),
                lambda E: E.tensor_sub(lnsc[:, 5:6], lnsc[:, 3:4], lnsc[:, 4:5]),
                lambda E: E.tensor_scalar(lnsc[:, 5:6], lnsc[:, 5:6], EPS_LN, None, ALU.add),
                lambda E: E.reciprocal(lnsc[:, 6:7], lnsc[:, 5:6]),
            ])
            wait(p, "scalar", s_v, g_rv)
            g_st = inc(p, "scalar", s_s, lambda E: E.activation(lnsc[:, 7:8], lnsc[:, 6:7], AF.Sqrt))
            wait(p, "vector", s_s, g_st)
            gv = inc(p, "vector", s_v, lambda E: E.tensor_scalar(
                xn_bf_out, x_tile, lnsc[:, 1:2], lnsc[:, 7:8], ALU.subtract, ALU.mult))
            return gv, g_st

        def emit_t16(p, trps, src_bf, dst_col_fn, idn=None):
            if idn is None:
                idn = ident
            B = trps.shape[1] // 128
            nb = NKC // B
            for batch in range(nb):
                wait(p, "tensor", s_v, T["trps_free"])
                for j in range(B - 1):
                    kc = batch * B + j
                    op(p, "tensor", lambda E, kc=kc, j=j: E.transpose(
                        trps[:, j * 128:(j + 1) * 128], src_bf[:, kc * 128:(kc + 1) * 128], idn[:]))
                kc = batch * B + B - 1
                gt = inc(p, "tensor", s_t, lambda E, kc=kc, B=B: E.transpose(
                    trps[:, (B - 1) * 128:B * 128], src_bf[:, kc * 128:(kc + 1) * 128], idn[:]))
                wait(p, "vector", s_t, gt)
                dst = dst_col_fn(batch, B)
                T["trps_free"] = inc(p, "vector", s_v, lambda E, dst=dst, B=B: E.tensor_copy(
                    dst, trps[:, 0:B * 128].rearrange("p (a b) -> p a b", a=B)))

        p0 = Phase()
        dma(p0, "sync", s_x, bq_sb[:], bq_r)
        dma(p0, "sync", s_x, bk_sb[:], bk_r)
        dma(p0, "sync", s_x, bv_sb[:], bv_r)
        dma(p0, "sync", s_x, ident[:], ident_in)
        dma(p0, "gpsimd", s_w, wmq_bf[:], wmap_q)
        dma(p0, "gpsimd", s_w, wmk_bf[:], wmap_k)
        dma(p0, "gpsimd", s_w, gateT_bf[:], gate_img)
        dma(p0, "gpsimd", s_w, shg1_bf[:], shg1_img)
        op(p0, "gpsimd", lambda E: E.iota(iota_sb[:], pattern=[[1, NE]], base=0,
                                          channel_multiplier=0,
                                          allow_small_or_imprecise_dtypes=True))
        op(p0, "vector", lambda E: E.memset(ones_bf[:], 1.0))
        for e in ("vector", "scalar", "tensor"):
            wait(p0, e, s_x)
            wait(p0, e, s_w)
        barrier(p0)
        run(p0)

        if upto >= 1:
          with contextlib.ExitStack() as ph:
            psb = lambda n, s, d: ph.enter_context(nc.psum_tensor(n, s, d))
            lsb = lambda n, s, d: ph.enter_context(nc.sbuf_tensor(n, s, d))
            kv_ps = psb("kv_ps", [64, NH, 128], F32)
            trps = psb("trps", [128, 1024], BF16)
            small2 = psb("small2", [128, 512], F32)
            k_ps = psb("k_ps", [128, 512], F32)
            v_ps = psb("v_ps", [128, 512], F32)
            k2_ps = small2[:, 0:256]
            kst = small2[0:64, 256:257]

            x_t2 = lsb("x_t2", [128, 2, HID], F32)
            xn_cast = lsb("xn_cast", [128, 2, HID], BF16)
            v_sb = lsb("v_sb", [128, 2, 512], BF16)
            ksum_acc = lsb("ksum_acc", [64, NH], F32)
            T["xsq"] = lsb("ln_xsq1", [128, HID], F32)
            xn_tb = lsb("xn_tb", [128, 2, NKC, 512], BF16)
            wk_sl = lsb("wk_sl", [128, 3, NKC, 128], BF16)
            wv_sl = lsb("wv_sl", [128, 3, NKC, 128], BF16)
            k_sb = lsb("k_sb", [128, 2, 512], BF16)
            v_tm = lsb("v_tm", [128, 2, 4, 128], BF16)
            k2_sb = lsb("k2_sb", [128, 2, 4, DIM], BF16)
            et = lsb("et", [128, 3, 256], F32)

            p1 = Phase()
            phase_sync(p1)
            kv_started = [False] * 4
            xt_free = [(0, 0), (0, 0)]
            wk_free = [0, 0, 0]
            wv_free = [0, 0, 0]
            XN_READY = [0, 0]
            XN_FREE = [0, 0]
            xchunk = [0]
            H = {}

            XLD = {}
            XLN = {}
            xc_free = [0, 0]
            lnchunk = [0]

            def emit_x_load(tb, ch, own=False):
                bi = xchunk[0] % 2
                xchunk[0] += 1
                fv, fs = xt_free[bi]
                wait(p1, "sync", s_v, fv)
                wait(p1, "sync", s_s, fs)
                r0 = ch * 128 if own else tb * 512 + ch * 128
                src = x_own if own else x_full
                XLD[(tb, ch, own)] = (bi, dma(p1, "sync", s_xp[bi], x_t2[:, bi, :], src[r0:r0 + 128, :]))

            def emit_ln_part(tb, ch, own=False):
                bi, ld = XLD.pop((tb, ch, own))
                cb = lnchunk[0] % 2
                lnchunk[0] += 1
                wait(p1, "vector", s_xp[bi], ld)
                wait(p1, "scalar", s_xp[bi], ld)
                wait(p1, "vector", s_t, xc_free[cb])
                gv, gs = emit_ln(p1, x_t2[:, bi, :], xn_cast[:, cb, :])
                xt_free[bi] = (gv, gs)
                XLN[(tb, ch, own)] = (cb, gv)

            def emit_trps_part(tb, ch, own=False):
                par = tb % 2
                cb, gv = XLN.pop((tb, ch, own))
                if ch == 0 and not own:
                    wait(p1, "vector", s_t, XN_FREE[par])
                wait(p1, "tensor", s_v, gv)
                if own:
                    dst = lambda batch, B, ch=ch: xn_own[:, batch * B:(batch + 1) * B,
                                                        ch * 128:(ch + 1) * 128]
                else:
                    dst = lambda batch, B, ch=ch, par=par: xn_tb[:, par, batch * B:(batch + 1) * B,
                                                                 ch * 128:(ch + 1) * 128]
                emit_t16(p1, trps[:, 0:1024] if False else trps, xn_cast[:, cb, :], dst)
                xc_free[cb] = s_t.n

            def emit_stageA_chunk(tb, ch):
                emit_x_load(tb, ch)
                emit_ln_part(tb, ch)
                emit_trps_part(tb, ch)

            def emit_ldkv(key, h, gi):
                bi = gi % 3
                d = H.setdefault(key, {})
                d["bi"] = bi
                wait(p1, "gpsimd", s_t, wk_free[bi])
                d["ldk"] = dma(p1, "gpsimd", s_wk[bi], wk_sl[:, bi], wk_img[h])
                wait(p1, "gpsimd", s_t, wv_free[bi])
                d["ldv"] = dma(p1, "gpsimd", s_wv[bi], wv_sl[:, bi], wv_img[h])

            def emit_kvproj(key, prev):
                tb, h = key
                hp = h % 2
                par = tb % 2
                d = H[key]
                bi = d["bi"]
                wait(p1, "tensor", s_wk[bi], d["ldk"])
                wait(p1, "tensor", s_v, XN_READY[par])
                if prev is not None:
                    wait(p1, "tensor", s_s, H[prev]["ksb"])
                for kc in range(NKC - 1):
                    op(p1, "tensor", lambda E, kc=kc, bi=bi, par=par: E.matmul(
                        k_ps[:], wk_sl[:, bi, kc, :], xn_tb[:, par, kc, :],
                        start=(kc == 0), stop=False))
                wk_free[bi] = inc(p1, "tensor", s_t, lambda E, bi=bi, par=par: E.matmul(
                    k_ps[:], wk_sl[:, bi, NKC - 1, :], xn_tb[:, par, NKC - 1, :],
                    start=False, stop=True))
                d["kps"] = s_t.n
                wait(p1, "scalar", s_t, d["kps"])
                d["ksb"] = inc(p1, "scalar", s_s, lambda E, h=h, hp=hp: E.activation(
                    k_sb[:, hp, :], k_ps[:], AF.Identity, bias=bk_sb[:, h:h + 1]))
                wait(p1, "tensor", s_wv[bi], d["ldv"])
                if prev is not None:
                    wait(p1, "tensor", s_s, H[prev]["vsb"])
                for kc in range(NKC - 1):
                    op(p1, "tensor", lambda E, kc=kc, bi=bi, par=par: E.matmul(
                        v_ps[:], wv_sl[:, bi, kc, :], xn_tb[:, par, kc, :],
                        start=(kc == 0), stop=False))
                wv_free[bi] = inc(p1, "tensor", s_t, lambda E, bi=bi, par=par: E.matmul(
                    v_ps[:], wv_sl[:, bi, NKC - 1, :], xn_tb[:, par, NKC - 1, :],
                    start=False, stop=True))
                d["vps"] = s_t.n
                wait(p1, "scalar", s_t, d["vps"])
                d["vsb"] = inc(p1, "scalar", s_s, lambda E, h=h, hp=hp: E.activation(
                    v_sb[:, hp, :], v_ps[:], AF.Identity, bias=bv_sb[:, h:h + 1]))

            def emit_k2elu(key, prev):
                tb, h = key
                hp = h % 2
                d = H[key]
                wait(p1, "tensor", s_s, d["ksb"])
                pv = max(H[prev]["k2sb"], H[prev]["vtm"], H[prev]["ksadd"]) if prev is not None else 0
                wait(p1, "tensor", s_v, pv)
                for sc4 in range(3):
                    op(p1, "tensor", lambda E, sc4=sc4, st=(sc4 == 0), hp=hp: E.matmul(
                        k2_ps[:, sc4 * 64:(sc4 + 1) * 64],
                        k_sb[:, hp, sc4 * 128:(sc4 + 1) * 128], wmk_bf[:],
                        start=st, stop=True, skip_group_check=True))
                g_k2 = inc(p1, "tensor", s_t, lambda E, hp=hp: E.matmul(
                    k2_ps[:, 3 * 64:4 * 64],
                    k_sb[:, hp, 3 * 128:4 * 128], wmk_bf[:],
                    start=False, stop=True, skip_group_check=True))
                wait(p1, "scalar", s_t, g_k2)
                gs1 = inc(p1, "scalar", s_s, lambda E: E.activation(et[:, 1, :], k2_ps[:], AF.Exp))
                wait(p1, "vector", s_t, g_k2)
                wait(p1, "vector", s_s, gs1)
                d["elu_rd"] = seq(p1, "vector", [lambda E: E.tensor_scalar(
                    et[:, 2, :], k2_ps[:], 0.0, None, ALU.max)])
                d["k2sb"] = inc(p1, "vector", s_v, lambda E, hp=hp: E.scalar_tensor_tensor(
                    k2_sb[:, hp, :, :].rearrange("p a b -> p (a b)"), et[:, 1, :], 1.0,
                    et[:, 2, :], ALU.min, ALU.add))

            def emit_vt(key):
                tb, h = key
                hp = h % 2
                d = H[key]
                wait(p1, "tensor", s_s, d["vsb"])
                wait(p1, "tensor", s_v, T["trps_free"])
                for j in range(3):
                    op(p1, "tensor", lambda E, j=j, hp=hp: E.transpose(
                        trps[:, j * 128:(j + 1) * 128], v_sb[:, hp, j * 128:(j + 1) * 128], ident[:]))
                g_vt = inc(p1, "tensor", s_t, lambda E, hp=hp: E.transpose(
                    trps[:, 3 * 128:4 * 128], v_sb[:, hp, 3 * 128:4 * 128], ident[:]))
                wait(p1, "vector", s_t, g_vt)
                d["vtm"] = inc(p1, "vector", s_v, lambda E, hp=hp: E.tensor_copy(
                    v_tm[:, hp, :, :].rearrange("p a b -> p (a b)"), trps[:, 0:512]))
                T["trps_free"] = d["vtm"]

            def emit_kv(key):
                tb, h = key
                hp = h % 2
                d = H[key]
                bank = h // 4
                wait(p1, "tensor", s_v, max(d["k2sb"], d["vtm"]))
                for sc4 in range(4):
                    op(p1, "tensor", lambda E, h=h, sc4=sc4, st=not kv_started[bank], hp=hp: E.matmul(
                        kv_ps[:, h, :], k2_sb[:, hp, sc4, :], v_tm[:, hp, sc4, :],
                        start=st, stop=True, skip_group_check=True))
                    kv_started[bank] = True
                    if sc4 < 3:
                        op(p1, "tensor", lambda E, sc4=sc4, st=(sc4 == 0), hp=hp: E.matmul(
                            kst[:], k2_sb[:, hp, sc4, :], ones_bf[:],
                            start=st, stop=True, skip_group_check=True))
                g_ks = inc(p1, "tensor", s_t, lambda E, hp=hp: E.matmul(
                    kst[:], k2_sb[:, hp, 3, :], ones_bf[:],
                    start=False, stop=True, skip_group_check=True))
                wait(p1, "vector", s_t, g_ks)
                d["ksadd"] = seq(p1, "vector", [lambda E, h=h: E.tensor_add(
                    ksum_acc[:, h:h + 1], ksum_acc[:, h:h + 1], kst[:])])

            seq(p1, "vector", [lambda E: E.memset(ksum_acc[:], 0.0)])
            for ch in range(4):
                emit_stageA_chunk(0, ch)
            XN_READY[0] = s_v.n

            own_done = set()
            heads = [(tb, h) for tb in range(ntb) for h in range(nh)]
            emit_ldkv(heads[0], heads[0][1], 0)
            if len(heads) > 1:
                emit_ldkv(heads[1], heads[1][1], 1)
            emit_kvproj(heads[0], None)
            for i, key in enumerate(heads):
                tb, h = key
                prev = heads[i - 1] if i > 0 else None
                full = (nh == NH)
                if full:
                    if tb + 1 < ntb and h in (6, 7, 9, 11):
                        emit_x_load(tb + 1, {6: 0, 7: 1, 9: 2, 11: 3}[h])
                    if tb in (1, 2) and h in (0, 2, 4, 6):
                        own_done.add((tb - 1) * 4 + h // 2)
                        emit_x_load(tb, (tb - 1) * 4 + h // 2, own=True)
                if i + 2 < len(heads):
                    emit_ldkv(heads[i + 2], heads[i + 2][1], i + 2)
                emit_k2elu(key, prev)
                emit_vt(key)
                if full:
                    if tb + 1 < ntb and h in (8, 10, 12, 14):
                        emit_ln_part(tb + 1, (h - 8) // 2)
                    if tb in (1, 2) and h in (1, 3, 5, 7):
                        emit_ln_part(tb, (tb - 1) * 4 + (h - 1) // 2, own=True)
                if i + 1 < len(heads):
                    emit_kvproj(heads[i + 1], key)
                emit_kv(key)
                if full:
                    if tb + 1 < ntb and h in (8, 10, 12, 14):
                        emit_trps_part(tb + 1, (h - 8) // 2)
                        if h == 14:
                            XN_READY[(tb + 1) % 2] = s_v.n
                    if tb in (1, 2) and h in (1, 3, 5, 7):
                        emit_trps_part(tb, (tb - 1) * 4 + (h - 1) // 2, own=True)
                if h == nh - 1:
                    XN_FREE[tb % 2] = H[key]["vps"]
                if nh < NH and h == nh - 1 and tb + 1 < ntb:
                    for ch in range(4):
                        emit_stageA_chunk(tb + 1, ch)
                    XN_READY[(tb + 1) % 2] = s_v.n
            for ch in range(NT):
                if ch not in own_done:
                    emit_x_load(0, ch, own=True)
                    emit_ln_part(0, ch, own=True)
                    emit_trps_part(0, ch, own=True)

            wait(p1, "vector", s_t, s_t.n)
            inc(p1, "vector", s_v, lambda E: E.tensor_copy(
                kv_bf[:, 0:nh, :].rearrange("p a b -> p (a b)"),
                kv_ps[:, 0:nh, :].rearrange("p a b -> p (a b)")))
            inc(p1, "vector", s_v, lambda E: E.tensor_copy(ksum_bf[:, 0:nh], ksum_acc[:, 0:nh]))
            if debug:
                wait(p1, "sync", s_v, s_v.n)
                dma(p1, "sync", s_dbg, dbg["kv"][:, 0:nh, :].rearrange("p a b -> p (a b)") if nh < NH else dbg["kv"].rearrange("p a b -> p (a b)"), kv_bf[:, 0:nh, :].rearrange("p a b -> p (a b)"))
                dma(p1, "sync", s_dbg, dbg["ksum"][:, 0:nh] if nh < NH else dbg["ksum"], ksum_bf[:, 0:nh])
                wait(p1, "sync", s_dbg)
            barrier(p1)
            run(p1)
        T["trps_free"] = 0
        T["xn_free"] = 0

        acc = sb("acc", [128, NT, HID], F32)
        if upto >= 2:
          with contextlib.ExitStack() as ph:
            lsb = lambda n, s, d: ph.enter_context(nc.sbuf_tensor(n, s, d))
            attn_fm = lsb("attn_fm", [128, NKC, 1024], BF16)

            if debug:
                pxa = Phase()
                dma(pxa, "sync", s_dbg, dbg["xn"], xn_own[:])
                wait(pxa, "sync", s_dbg)
                run(pxa)

            with (
                nc.psum_tensor("out_ps", [128, HID], F32) as out_ps,
                nc.psum_tensor("small3", [128, 512], F32) as small3,
                nc.psum_tensor("trps3", [128, 1024], BF16) as trps,
            ):
                nm_ps = small3[:, 0:16]
                q2h = ph.enter_context(nc.sbuf_tensor("q2h", [64, 2, NH, 512], BF16))
                attn_tm = ph.enter_context(nc.sbuf_tensor("attn_tm", [128, HID], BF16))
                RQ = []
                rq_i = [0]
                SEMS = {"tensor": s_t, "vector": s_v, "scalar": s_s}

                def queue_routing(t, src, bufs):
                    rwork, rsel, rtmp, rex = bufs
                    RQ.append(("vector", [lambda E: E.tensor_copy(rwork[:], src)], ()))
                    RQ.append(("vector", [lambda E: E.memset(rsel[:], 0.0)], ()))
                    RQ.append(("vector", [lambda E: E.reduce_max(
                        rt_sc[:, 1:2], src, axis=AX, negate=True)], ()))
                    for j in range(TOPK):
                        for b in [
                            lambda E: E.reduce_max(rt_sc[:, 0:1], rwork[:], axis=AX),
                            lambda E: E.tensor_scalar(rtmp[:], rwork[:], rt_sc[:, 0:1], None, ALU.is_equal),
                            lambda E: E.tensor_scalar(rtmp[:], rtmp[:], -BIG, BIG, ALU.mult, ALU.add),
                            lambda E: E.tensor_add(rtmp[:], rtmp[:], iota_sb[:]),
                            lambda E: E.tensor_reduce(rt_sc[:, 2:3], rtmp[:], AX, ALU.min),
                            lambda E: E.tensor_scalar(rtmp[:], iota_sb[:], rt_sc[:, 2:3], None, ALU.is_equal),
                            lambda E: E.tensor_add(rsel[:], rsel[:], rtmp[:]),
                            lambda E: E.scalar_tensor_tensor(rwork[:], rtmp[:], -BIG, rwork[:], ALU.mult, ALU.add),
                        ]:
                            RQ.append(("vector", [b], ()))
                    RQ.append(("scalar", [lambda E: E.activation(
                        rex[:], src, AF.Exp, bias=rt_sc[:, 1:2])], ("vector",)))
                    RQ.append(("vector", [lambda E: E.tensor_mul(rex[:], rex[:], rsel[:])], ("scalar",)))
                    RQ.append(("vector", [lambda E: E.reduce_sum(rt_sc[:, 3:4], rex[:], axis=AX)], ()))
                    RQ.append(("vector", [lambda E: E.reciprocal(rt_sc[:, 4:5], rt_sc[:, 3:4])], ()))
                    RQ.append(("vector", [lambda E, t=t: E.tensor_scalar(
                        W_sb[:, t, :], rex[:], rt_sc[:, 4:5], None, ALU.mult)], ()))

                def drip(p, n):
                    end = min(rq_i[0] + n, len(RQ))
                    while rq_i[0] < end:
                        eng, builders, wengs = RQ[rq_i[0]]
                        for we in wengs:
                            wait(p, eng, SEMS[we])
                        for b in builders[:-1]:
                            op(p, eng, b)
                        g = inc(p, eng, SEMS[eng], builders[-1])
                        if eng == "vector":
                            wait(p, "vector", s_v, g)
                        rq_i[0] += 1

                atn_ready = [0] * 8
                C2 = dict(rn_rd=0, evac=0)
                AT = {}

                def attn_p1(p, half, t):
                    wait(p, "tensor", s_v, C2["rn_rd"])
                    for h in range(NH - 1):
                        op(p, "tensor", lambda E, h=h, t=t, half=half: E.matmul(
                            nm_ps[:, h:h + 1], q2h[:, half, h, t * 128:(t + 1) * 128],
                            ksum_bf[:, h:h + 1], start=(h == 0), stop=True,
                            skip_group_check=True))
                    g_nm = inc(p, "tensor", s_t, lambda E, t=t, half=half: E.matmul(
                        nm_ps[:, NH - 1:NH], q2h[:, half, NH - 1, t * 128:(t + 1) * 128],
                        ksum_bf[:, NH - 1:NH], start=False, stop=True,
                        skip_group_check=True))
                    wait(p, "vector", s_t, g_nm)
                    rn = AT[(half, t)]["rn"]
                    g_rn = seq(p, "vector", [
                        (lambda rn: lambda E: E.tensor_scalar(
                            rn, nm_ps[:, 0:NH], EPS_AT, None, ALU.add))(rn),
                        (lambda rn: lambda E: E.reciprocal(rn, rn))(rn),
                    ])
                    C2["rn_rd"] = g_rn
                    AT[(half, t)]["g_rn"] = g_rn

                def attn_p2(p, half, t):
                    wait(p, "tensor", s_v, C2["evac"])
                    for h in range(NH - 1):
                        op(p, "tensor", lambda E, h=h, t=t, half=half: E.matmul(
                            out_ps[:, h * 128:(h + 1) * 128],
                            q2h[:, half, h, t * 128:(t + 1) * 128], kv_bf[:, h, :],
                            start=(h % 4 == 0), stop=True, skip_group_check=True))
                    g_out = inc(p, "tensor", s_t, lambda E, t=t, half=half: E.matmul(
                        out_ps[:, (NH - 1) * 128:NH * 128],
                        q2h[:, half, NH - 1, t * 128:(t + 1) * 128], kv_bf[:, NH - 1, :],
                        start=False, stop=True, skip_group_check=True))
                    d = AT[(half, t)]
                    wait(p, "vector", s_t, g_out)
                    wait(p, "vector", s_v, d["g_rn"])
                    rn = d["rn"]
                    for h in range(NH - 1):
                        op(p, "vector", lambda E, h=h, rn=rn: E.tensor_scalar(
                            attn_tm[:, h * 128:(h + 1) * 128],
                            out_ps[:, h * 128:(h + 1) * 128],
                            rn[:, h:h + 1], None, ALU.mult))
                    g_ev = inc(p, "vector", s_v, lambda E, rn=rn: E.tensor_scalar(
                        attn_tm[:, (NH - 1) * 128:NH * 128],
                        out_ps[:, (NH - 1) * 128:NH * 128],
                        rn[:, NH - 1:NH], None, ALU.mult))
                    C2["evac"] = g_ev
                    d["g_ev"] = g_ev

                def attn_p3(p, half, t):
                    ti = half * 4 + t
                    d = AT.pop((half, t))
                    wait(p, "tensor", s_v, d["g_ev"])
                    emit_t16(p, trps, attn_tm,
                             lambda batch, B, ti=ti: attn_fm[:, batch * B:(batch + 1) * B,
                                                            ti * 128:(ti + 1) * 128])
                    atn_ready[ti] = s_v.n

                with contextlib.ExitStack() as qs:
                    wq_sl = qs.enter_context(nc.sbuf_tensor("wq_sl", [128, 3, NKC, 128], BF16))
                    qf_sb = qs.enter_context(nc.sbuf_tensor("qf_sb", [128, 2, 512], BF16))
                    etq = qs.enter_context(nc.sbuf_tensor("etq", [128, 2, 512], F32))
                    lg_sb4 = qs.enter_context(nc.sbuf_tensor("lg_sb4", [128, 4, NE], F32))
                    rbufs_q = tuple(qs.enter_context(nc.sbuf_tensor(n, [128, NE], F32))
                                    for n in ("rwork_q", "rsel_q", "rtmp_q", "rex_q"))
                    q_ps = qs.enter_context(nc.psum_tensor("q_ps", [128, 512], F32))
                    q2_ps = qs.enter_context(nc.psum_tensor("q2_ps", [64, 512], F32))

                    for half in range(2):
                        p2b = Phase()
                        phase_sync(p2b)
                        if half == 0 and upto >= 3:
                            g_lg4 = 0
                            for t in range(4):
                                bs = t * 128
                                for kc in range(NKC):
                                    op(p2b, "tensor", lambda E, kc=kc, t=t, bs=bs: E.matmul(
                                        out_ps[:, bs:bs + NE],
                                        xn_own[:, kc, t * 128:(t + 1) * 128],
                                        gateT_bf[:, kc, :],
                                        start=(t == 0 and kc == 0),
                                        stop=(kc == NKC - 1), skip_group_check=True))
                                for kc in range(NKC):
                                    bld = lambda E, kc=kc, t=t, bs=bs: E.matmul(
                                        out_ps[:, bs + 64:bs + 65],
                                        xn_own[:, kc, t * 128:(t + 1) * 128],
                                        shg1_bf[:, kc, :], start=False,
                                        stop=(kc == NKC - 1), skip_group_check=True)
                                    if t == 3 and kc == NKC - 1:
                                        g_lg4 = inc(p2b, "tensor", s_t, bld)
                                    else:
                                        op(p2b, "tensor", bld)
                            wait(p2b, "vector", s_t, g_lg4)
                            for t in range(4):
                                bld = lambda E, t=t: E.tensor_copy(
                                    lg_sb4[:, t, :], out_ps[:, t * 128:t * 128 + NE])
                                if t == 3:
                                    inc(p2b, "vector", s_v, bld)
                                else:
                                    op(p2b, "vector", bld)
                            wait(p2b, "scalar", s_t, g_lg4)
                            for t in range(4):
                                op(p2b, "scalar", lambda E, t=t: E.activation(
                                    shgate_sb[:, t:t + 1],
                                    out_ps[:, t * 128 + 64:t * 128 + 65], AF.Sigmoid))
                            for t in range(4):
                                queue_routing(t, lg_sb4[:, t, :], rbufs_q)
                        wq_free = [0, 0, 0]
                        QH = {}

                        def q_ldkv(h):
                            bi = h % 3
                            wait(p2b, "gpsimd", s_t, wq_free[bi])
                            QH.setdefault(h, {})["ldq"] = dma(
                                p2b, "gpsimd", s_wq[bi], wq_sl[:, bi], wq_img[h])

                        def q_proj(h):
                            bi = h % 3
                            d = QH[h]
                            wait(p2b, "tensor", s_wq[bi], d["ldq"])
                            if h >= 1:
                                wait(p2b, "tensor", s_s, QH[h - 1]["qf"])
                            for kc in range(NKC - 1):
                                op(p2b, "tensor", lambda E, kc=kc, bi=bi, half=half: E.matmul(
                                    q_ps[:], wq_sl[:, bi, kc, :],
                                    xn_own[:, kc, half * 512:(half + 1) * 512],
                                    start=(kc == 0), stop=False))
                            wq_free[bi] = inc(p2b, "tensor", s_t, lambda E, bi=bi, half=half: E.matmul(
                                q_ps[:], wq_sl[:, bi, NKC - 1, :],
                                xn_own[:, NKC - 1, half * 512:(half + 1) * 512],
                                start=False, stop=True))
                            wait(p2b, "scalar", s_t, s_t.n)
                            d["qf"] = inc(p2b, "scalar", s_s, lambda E, h=h, qb=h % 2: E.activation(
                                qf_sb[:, qb, :], q_ps[:], AF.Identity, bias=bq_sb[:, h:h + 1]))

                        def q_map(h):
                            d = QH[h]
                            wait(p2b, "tensor", s_s, d["qf"])
                            if h >= 1:
                                wait(p2b, "tensor", s_v, QH[h - 1]["q2sb"])
                            g_q2 = inc(p2b, "tensor", s_t, lambda E, qb=h % 2: E.matmul(
                                q2_ps[:], wmq_bf[:], qf_sb[:, qb, :], start=True, stop=True))
                            wait(p2b, "scalar", s_t, g_q2)
                            gs1 = inc(p2b, "scalar", s_s, lambda E: E.activation(
                                etq[0:64, 1, :], q2_ps[:], AF.Exp))
                            wait(p2b, "vector", s_t, g_q2)
                            wait(p2b, "vector", s_s, gs1)
                            seq(p2b, "vector", [lambda E: E.tensor_scalar(
                                etq[0:64, 0, :], q2_ps[:], 0.0, None, ALU.max)])
                            d["q2sb"] = inc(p2b, "vector", s_v, lambda E, h=h, half=half: E.scalar_tensor_tensor(
                                q2h[:, half, h, :], etq[0:64, 1, :], 1.0,
                                etq[0:64, 0, :], ALU.min, ALU.add))

                        q_ldkv(0)
                        q_ldkv(1)
                        q_proj(0)
                        for h in range(NH):
                            if h + 2 < NH:
                                q_ldkv(h + 2)
                            if h + 1 < NH:
                                q_proj(h + 1)
                            q_map(h)
                            if upto >= 3:
                                drip(p2b, 9)
                            if half == 1:
                                t4, ph4 = h // 4, h % 4
                                if ph4 == 1:
                                    AT[(0, t4)] = {"rn": rn_sc[:, 0:NH] if t4 % 2 == 0 else rn2_sc[:, 0:NH]}
                                    attn_p1(p2b, 0, t4)
                                elif ph4 == 2:
                                    attn_p2(p2b, 0, t4)
                                elif ph4 == 3:
                                    attn_p3(p2b, 0, t4)
                        if upto >= 3 and half == 1:
                            drip(p2b, len(RQ))
                        barrier(p2b)
                        run(p2b)

                with contextlib.ExitStack() as hh:
                    woq = hh.enter_context(nc.sbuf_tensor("woq", [128, 2, NKC, 512], BF16))
                    lg_sb = hh.enter_context(nc.sbuf_tensor("lg_sb", [128, NE], F32))
                    rbufs_h = tuple(hh.enter_context(nc.sbuf_tensor(n, [128, NE], F32))
                                    for n in ("rwork_h", "rsel_h", "rtmp_h", "rex_h"))
                    lg_ps = small3[:, 128:256]
                    with nc.psum_tensor("wo_ps", [128, 2, 512], F32) as wo_ps:
                        p2d = Phase()
                        phase_sync(p2d)
                        wo_ld = [0, 0, 0, 0]
                        wo_ld[0] = dma(p2d, "gpsimd", s_wo2[0], woq[:, 0], wo_img[0])
                        wo_ld[1] = dma(p2d, "gpsimd", s_wo2[1], woq[:, 1], wo_img[1])
                        if upto >= 3:
                            for t in range(4, NT):
                                mms = []
                                for kc in range(NKC):
                                    mms.append(lambda E, kc=kc, t=t: E.matmul(
                                        lg_ps[:, 0:NE], xn_own[:, kc, t * 128:(t + 1) * 128],
                                        gateT_bf[:, kc, :], start=(kc == 0),
                                        stop=(kc == NKC - 1), skip_group_check=True))
                                for kc in range(NKC):
                                    mms.append(lambda E, kc=kc, t=t: E.matmul(
                                        lg_ps[:, 64:65], xn_own[:, kc, t * 128:(t + 1) * 128],
                                        shg1_bf[:, kc, :], start=False,
                                        stop=(kc == NKC - 1), skip_group_check=True))
                                RQ.append(("tensor", mms, ("vector", "scalar")))
                                RQ.append(("vector", [lambda E: E.tensor_copy(
                                    lg_sb[:], lg_ps[:, 0:NE])], ("tensor",)))
                                RQ.append(("scalar", [lambda E, t=t: E.activation(
                                    shgate_sb[:, t:t + 1], lg_ps[:, 64:65],
                                    AF.Sigmoid)], ("tensor",)))
                                queue_routing(t, lg_sb[:], rbufs_h)
                        wo_free = [0, 0]
                        ps_free = [0, 0]
                        at_stage = [0]

                        def maybe_attn():
                            st = at_stage[0]
                            if st >= 12:
                                return
                            t4, phs = st // 3, st % 3
                            if phs == 0:
                                AT[(1, t4)] = {"rn": rn_sc[:, 0:NH] if t4 % 2 == 0 else rn2_sc[:, 0:NH]}
                                attn_p1(p2d, 1, t4)
                            elif phs == 1:
                                attn_p2(p2d, 1, t4)
                            else:
                                attn_p3(p2d, 1, t4)
                            at_stage[0] += 1

                        for qt in range(4):
                            bi = qt % 2
                            if qt >= 2:
                                wait(p2d, "gpsimd", s_t, wo_free[bi])
                                wo_ld[qt] = dma(p2d, "gpsimd", s_wo2[bi], woq[:, bi], wo_img[qt])
                            wait(p2d, "tensor", s_wo2[bi], wo_ld[qt])
                            for t in range(NT):
                                if upto >= 3 and qt >= 1:
                                    drip(p2d, 13)
                                if qt == 0:
                                    maybe_attn()
                                    maybe_attn()
                                if t >= 4:
                                    wait(p2d, "tensor", s_v, atn_ready[t])
                                pp = (qt * NT + t) % 2
                                wait(p2d, "tensor", s_v, ps_free[pp])
                                for kc in range(NKC - 1):
                                    op(p2d, "tensor", lambda E, kc=kc, t=t, bi=bi, pp=pp: E.matmul(
                                        wo_ps[:, pp, :], attn_fm[:, kc, t * 128:(t + 1) * 128],
                                        woq[:, bi, kc, :], start=(kc == 0), stop=False))
                                g_mm = inc(p2d, "tensor", s_t, lambda E, t=t, bi=bi, pp=pp: E.matmul(
                                    wo_ps[:, pp, :], attn_fm[:, NKC - 1, t * 128:(t + 1) * 128],
                                    woq[:, bi, NKC - 1, :], start=False, stop=True))
                                if t == NT - 1:
                                    wo_free[bi] = g_mm
                                wait(p2d, "vector", s_t, g_mm)
                                ps_free[pp] = inc(p2d, "vector", s_v, lambda E, t=t, qt=qt, pp=pp: E.tensor_copy(
                                    acc[:, t, qt * 512:(qt + 1) * 512], wo_ps[:, pp, :]))
                        if upto >= 3:
                            drip(p2d, len(RQ))
                        if debug:
                            wait(p2d, "sync", s_v, s_v.n)
                            wait(p2d, "sync", s_s, s_s.n)
                            dma(p2d, "sync", s_dbg, dbg["attn"], attn_fm[:])
                            dma(p2d, "sync", s_dbg, dbg["accw"], acc[:])
                            dma(p2d, "sync", s_dbg, dbg["W"], W_sb[:])
                            dma(p2d, "sync", s_dbg, dbg["shg"], shgate_sb[:])
                            wait(p2d, "sync", s_dbg)
                        barrier(p2d)
                        run(p2d)


        sh_state = {}
        if upto >= 4:
          sh_outer = contextlib.ExitStack()
          est.enter_context(sh_outer)
          shg_bf = sh_outer.enter_context(nc.sbuf_tensor("shg_bf", [128, NKC, NI], BF16))
          shu_bf = sh_outer.enter_context(nc.sbuf_tensor("shu_bf", [128, NKC, NI], BF16))
          with contextlib.ExitStack() as ph:
            lsb = lambda n, s, d: ph.enter_context(nc.sbuf_tensor(n, s, d))
            gu_sl = lsb("gu_sl", [128, 2, NKC, 256], BF16)
            dp_sl = lsb("dp_sl", [128, 2, 4, HID], BF16)
            inter = lsb("inter", [128, 4, 1024], BF16)
            sil_t = lsb("sil_t", [128, 1024], F32)
            with (
                nc.psum_tensor("gu_ps", [128, 2, 2, 512], F32) as gu_ps,
                nc.psum_tensor("dn_ps", [128, 2, 1024], F32) as dn_ps,
            ):
                p3b = Phase()
                phase_sync(p3b)
                gu_free = [0, 0]
                dp_free = [0, 0]
                dn_free = [0, 0]
                slot_sil = [0, 0]
                slot_int = [0, 0]
                g_int_prev = [0, 0, 0, 0]
                for e in range(ne):
                    dbi = e % 2
                    if e == max(ne - 6, 0):
                        sh_state["l1"] = dma(p3b, "gpsimd", s_w, shg_bf[:], shg_img)
                        sh_state["l2"] = dma(p3b, "gpsimd", s_w, shu_bf[:], shu_img)
                    wait(p3b, "gpsimd", s_t, dp_free[dbi])
                    ldd = dma(p3b, "gpsimd", s_dp[dbi], dp_sl[:, dbi], dp_img[e])
                    for k in range(4):
                        qbi = (e * 4 + k) % 2
                        wait(p3b, "gpsimd", s_t, gu_free[qbi])
                        ldg = dma(p3b, "gpsimd", s_gu[qbi], gu_sl[:, qbi], gu_img[e, k])
                        wait(p3b, "tensor", s_gu[qbi], ldg)
                        wait(p3b, "tensor", s_v, g_int_prev[k])
                        g_int = 0
                        for th in range(2):
                            slot = (k * 2 + th) % 2
                            wait(p3b, "tensor", s_s, slot_sil[slot])
                            wait(p3b, "tensor", s_v, slot_int[slot])
                            for kc in range(NKC):
                                op(p3b, "tensor", lambda E, kc=kc, th=th, qbi=qbi, slot=slot: E.matmul(
                                    gu_ps[:, slot, 0, :],
                                    gu_sl[:, qbi, kc, 0:128],
                                    xn_own[:, kc, th * 512:(th + 1) * 512],
                                    start=(kc == 0), stop=(kc == NKC - 1)))
                                if not (kc == NKC - 1):
                                    op(p3b, "tensor", lambda E, kc=kc, th=th, qbi=qbi, slot=slot: E.matmul(
                                        gu_ps[:, slot, 1, :],
                                        gu_sl[:, qbi, kc, 128:256],
                                        xn_own[:, kc, th * 512:(th + 1) * 512],
                                        start=(kc == 0), stop=(kc == NKC - 1)))
                            g_last = inc(p3b, "tensor", s_t, lambda E, th=th, qbi=qbi, slot=slot: E.matmul(
                                gu_ps[:, slot, 1, :],
                                gu_sl[:, qbi, NKC - 1, 128:256],
                                xn_own[:, NKC - 1, th * 512:(th + 1) * 512],
                                start=False, stop=True))
                            if th == 1:
                                gu_free[qbi] = g_last
                            wait(p3b, "scalar", s_t, g_last)
                            slot_sil[slot] = inc(p3b, "scalar", s_s, lambda E, slot=slot, th=th: E.activation(
                                sil_t[:, th * 512:(th + 1) * 512], gu_ps[:, slot, 0, :], AF.Silu))
                            wait(p3b, "vector", s_s, slot_sil[slot])
                            g_int = inc(p3b, "vector", s_v, lambda E, k=k, th=th, slot=slot: E.tensor_mul(
                                inter[:, k, th * 512:(th + 1) * 512],
                                sil_t[:, th * 512:(th + 1) * 512], gu_ps[:, slot, 1, :]))
                            slot_int[slot] = g_int
                        g_int_prev[k] = g_int
                    wait(p3b, "tensor", s_dp[dbi], ldd)
                    wait(p3b, "tensor", s_v, g_int_prev[3])
                    g_dn_last = 0
                    for t in range(NT):
                        for oh in range(2):
                            pp = (t * 2 + oh) % 2
                            wait(p3b, "tensor", s_v, dn_free[pp])
                            for k in range(4):
                                for o2 in range(2):
                                    if not (k == 3 and o2 == 1):
                                        op(p3b, "tensor", lambda E, k=k, o2=o2, oh=oh, t=t, dbi=dbi, pp=pp: E.matmul(
                                            dn_ps[:, pp, o2 * 512:(o2 + 1) * 512],
                                            inter[:, k, t * 128:(t + 1) * 128],
                                            dp_sl[:, dbi, k, oh * 1024 + o2 * 512: oh * 1024 + (o2 + 1) * 512],
                                            start=(k == 0), stop=(k == 3)))
                            g_dn = inc(p3b, "tensor", s_t, lambda E, oh=oh, t=t, dbi=dbi, pp=pp: E.matmul(
                                dn_ps[:, pp, 512:1024],
                                inter[:, 3, t * 128:(t + 1) * 128],
                                dp_sl[:, dbi, 3, oh * 1024 + 512: oh * 1024 + 1024],
                                start=False, stop=True))
                            g_dn_last = g_dn
                            wait(p3b, "vector", s_t, g_dn)
                            wait(p3b, "vector", s_v)
                            dn_free[pp] = inc(p3b, "vector", s_v, lambda E, t=t, oh=oh, e=e, pp=pp: E.scalar_tensor_tensor(
                                acc[:, t, oh * 1024:(oh + 1) * 1024], dn_ps[:, pp, :],
                                W_sb[:, t, e:e + 1], acc[:, t, oh * 1024:(oh + 1) * 1024],
                                ALU.mult, ALU.add))
                    dp_free[dbi] = g_dn_last
                barrier(p3b)
                run(p3b)

        if upto >= 5:
          with contextlib.ExitStack() as ph:
            lsb = lambda n, s, d: ph.enter_context(nc.sbuf_tensor(n, s, d))
            shd_bf = lsb("shd_bf", [128, 4, HID], BF16)
            inter = lsb("inter_sh", [128, 4, 1024], BF16)
            sil_t = lsb("sil_t2", [128, 1024], F32)
            with (
                nc.psum_tensor("sgu_ps", [128, 2, 1024], F32) as gu_ps,
                nc.psum_tensor("sdn_ps", [128, 2, 1024], F32) as dn_ps,
            ):
                p3c = Phase()
                phase_sync(p3c)
                sh_state["l3"] = dma(p3c, "gpsimd", s_wo2[0], shd_bf[:], shd_img)
                wait(p3c, "tensor", s_w, sh_state["l2"])
                g_int_prev = 0
                for k in range(4):
                    wait(p3c, "tensor", s_v, g_int_prev)
                    for kc in range(NKC):
                        for th in range(2):
                            op(p3c, "tensor", lambda E, kc=kc, th=th, k=k: E.matmul(
                                gu_ps[:, 0, th * 512:(th + 1) * 512],
                                shg_bf[:, kc, k * 128:(k + 1) * 128],
                                xn_own[:, kc, th * 512:(th + 1) * 512],
                                start=(kc == 0), stop=(kc == NKC - 1)))
                            if not (kc == NKC - 1 and th == 1):
                                op(p3c, "tensor", lambda E, kc=kc, th=th, k=k: E.matmul(
                                    gu_ps[:, 1, th * 512:(th + 1) * 512],
                                    shu_bf[:, kc, k * 128:(k + 1) * 128],
                                    xn_own[:, kc, th * 512:(th + 1) * 512],
                                    start=(kc == 0), stop=(kc == NKC - 1)))
                    g_gu = inc(p3c, "tensor", s_t, lambda E, k=k: E.matmul(
                        gu_ps[:, 1, 512:1024],
                        shu_bf[:, NKC - 1, k * 128:(k + 1) * 128],
                        xn_own[:, NKC - 1, 512:1024],
                        start=False, stop=True))
                    wait(p3c, "scalar", s_t, g_gu)
                    g_sil = inc(p3c, "scalar", s_s, lambda E: E.activation(
                        sil_t[:], gu_ps[:, 0, :], AF.Silu))
                    wait(p3c, "vector", s_s, g_sil)
                    g_int = inc(p3c, "vector", s_v, lambda E, k=k: E.tensor_mul(
                        inter[:, k, :], sil_t[:], gu_ps[:, 1, :]))
                    g_int_prev = g_int
                    wait(p3c, "tensor", s_s, g_sil)
                    wait(p3c, "tensor", s_v, g_int)
                dn_free = [0, 0]
                wait(p3c, "tensor", s_wo2[0], sh_state["l3"])
                for t in range(NT):
                    for oh in range(2):
                        pp = (t * 2 + oh) % 2
                        wait(p3c, "tensor", s_v, dn_free[pp])
                        for k in range(4):
                            for o2 in range(2):
                                if not (k == 3 and o2 == 1):
                                    op(p3c, "tensor", lambda E, k=k, o2=o2, oh=oh, t=t, pp=pp: E.matmul(
                                        dn_ps[:, pp, o2 * 512:(o2 + 1) * 512],
                                        inter[:, k, t * 128:(t + 1) * 128],
                                        shd_bf[:, k, oh * 1024 + o2 * 512: oh * 1024 + (o2 + 1) * 512],
                                        start=(k == 0), stop=(k == 3)))
                        g_dn = inc(p3c, "tensor", s_t, lambda E, oh=oh, t=t, pp=pp: E.matmul(
                            dn_ps[:, pp, 512:1024],
                            inter[:, 3, t * 128:(t + 1) * 128],
                            shd_bf[:, 3, oh * 1024 + 512: oh * 1024 + 1024],
                            start=False, stop=True))
                        wait(p3c, "vector", s_t, g_dn)
                        wait(p3c, "vector", s_v)
                        dn_free[pp] = inc(p3c, "vector", s_v, lambda E, t=t, oh=oh, pp=pp: E.scalar_tensor_tensor(
                            acc[:, t, oh * 1024:(oh + 1) * 1024], dn_ps[:, pp, :],
                            shgate_sb[:, t:t + 1], acc[:, t, oh * 1024:(oh + 1) * 1024],
                            ALU.mult, ALU.add))
                barrier(p3c)
                run(p3c)

        with contextlib.ExitStack() as ph:
            x_t2 = ph.enter_context(nc.sbuf_tensor("x_t2c", [128, 2, HID], F32))
            o_t2 = ph.enter_context(nc.sbuf_tensor("o_t2", [128, 2, HID], F32))
            bo_b = ph.enter_context(nc.sbuf_tensor("bo_b", [128, HID], F32))
            p4 = Phase()
            phase_sync(p4)
            if upto < 5:
                seq(p4, "vector", [lambda E: E.memset(acc[:].rearrange("p a b -> p (a b)"), 0.0)])
            ldb = dma(p4, "sync", s_x, bo_b[:], bo_bc)
            wait(p4, "vector", s_x, ldb)
            xf = [0, 0]
            of = [0, 0]
            for t in range(NT):
                bi = t % 2
                wait(p4, "sync", s_v, xf[bi])
                ld = dma(p4, "sync", s_xp[bi], x_t2[:, bi, :], x_own[t * 128:(t + 1) * 128, :])
                wait(p4, "vector", s_xp[bi], ld)
                wait(p4, "vector", s_op[bi], of[bi])
                xf[bi] = seq(p4, "vector", [
                    (lambda t, bi: lambda E: E.tensor_add(
                        o_t2[:, bi, :], acc[:, t, :], x_t2[:, bi, :]))(t, bi),
                    (lambda bi: lambda E: E.tensor_add(
                        o_t2[:, bi, :], o_t2[:, bi, :], bo_b[:]))(bi),
                ])
                wait(p4, "sync", s_v, xf[bi])
                of[bi] = dma(p4, "sync", s_op[bi], out_t[t * 128:(t + 1) * 128, :], o_t2[:, bi, :])
            wait(p4, "sync", s_op[0])
            wait(p4, "sync", s_op[1])
            wait(p4, "vector", s_op[0])
            wait(p4, "vector", s_op[1])
            run(p4)

    return nc


def prep_shared(ins):
    C = np.ascontiguousarray
    f = np.float32
    w = {}
    wqT = ins["L0_attn_wq"].T
    wkT = ins["L0_attn_wk"].T
    wvT = ins["L0_attn_wv"].T
    woT = ins["L0_attn_wo"].T
    img = lambda m: C(np.asarray(m, f).reshape(NKC, 128, NH, 128).transpose(2, 1, 0, 3))
    w["wq_img"] = img(wqT)
    w["wk_img"] = img(wkT)
    w["wv_img"] = img(wvT)
    w["wo_img"] = C(np.asarray(woT, f).reshape(NKC, 128, 4, 512).transpose(2, 1, 0, 3))
    w["bq_r"] = C(np.asarray(ins["L0_attn_bq"], f).reshape(NH, 128).T)
    w["bk_r"] = C(np.asarray(ins["L0_attn_bk"], f).reshape(NH, 128).T)
    w["bv_r"] = C(np.asarray(ins["L0_attn_bv"], f).reshape(NH, 128).T)
    w["bo_bc"] = C(np.broadcast_to(np.asarray(ins["L0_attn_bo"], f), (128, HID)))
    w["wmap_qT"] = C(np.asarray(ins["L0_attn_wmap_q"], f).T)
    w["wmap_kT"] = C(np.asarray(ins["L0_attn_wmap_k"], f).T)
    w["gate_img"] = C(np.asarray(ins["L0_ffn_gate_w"], f).T.reshape(NKC, 128, NE).transpose(1, 0, 2))
    guT = np.asarray(ins["L0_ffn_gu_w"], f).transpose(0, 2, 1)
    gu4 = guT.reshape(NE, HID, 8, 128)[:, :, [0, 4, 1, 5, 2, 6, 3, 7], :]
    gu4 = gu4.reshape(NE, NKC, 128, 4, 256).transpose(0, 3, 2, 1, 4)
    w["gu_img"] = C(gu4)
    dpT = np.asarray(ins["L0_ffn_dp_w"], f).transpose(0, 2, 1)
    w["dp_img"] = C(dpT.reshape(NE, 4, 128, HID).transpose(0, 2, 1, 3))
    shim = lambda m: C(np.asarray(m, f).T.reshape(NKC, 128, NI).transpose(1, 0, 2))
    w["shg_img"] = shim(ins["L0_ffn_sh_gate_w"])
    w["shu_img"] = shim(ins["L0_ffn_sh_up_w"])
    w["shd_img"] = C(np.asarray(ins["L0_ffn_sh_down_w"], f).T.reshape(4, 128, HID).transpose(1, 0, 2))
    w["shg1_img"] = C(np.asarray(ins["L0_ffn_sh_gate1_w"], f).T.reshape(NKC, 128, 1).transpose(1, 0, 2))
    w["ident_bf"] = np.eye(128, dtype=ml_dtypes.bfloat16)
    w["ident_f"] = np.eye(128, dtype=np.float32)
    return w


def make_in_maps(ins):
    shared = prep_shared(ins)
    x = np.asarray(ins["L0_attn_hidden_states"], np.float32)
    maps = []
    for c in range(8):
        b, qd = c // 4, c % 4
        m = dict(shared)
        m["x_full"] = np.ascontiguousarray(x[b])
        m["x_own"] = np.ascontiguousarray(x[b, qd * 1024:(qd + 1) * 1024])
        maps.append(m)
    return maps


def kernel(**inputs):
    maps = make_in_maps(inputs)
    nc = build_nc(debug=False)
    res = run_bass_kernel_spmd(nc, maps, core_ids=list(range(8)))
    out = np.concatenate([res.results[c]["out"] for c in range(8)], axis=0)
    return out.reshape(2, 4096, 2048)


# version 15
# speedup vs baseline: 1.0632x; 1.0525x over previous
import contextlib
import numpy as np
import ml_dtypes
import concourse.bass as bass
import concourse.mybir as mybir
from concourse.bass_utils import run_bass_kernel_spmd

F32, BF16 = mybir.dt.float32, mybir.dt.bfloat16
AF, ALU, AX = mybir.ActivationFunctionType, mybir.AluOpType, mybir.AxisListType.X

HID, S, NH, DIM, NE, NI = 2048, 4096, 16, 64, 64, 512
TOPK = 8
EPS_LN, EPS_AT = 1e-5, 1e-6
BIG = 1e30
NKC = HID // 128
NT = 8
NTB = 8

ENGS = ("sync", "gpsimd", "vector", "scalar", "tensor")


class Sem:
    __slots__ = ("h", "n")
    def __init__(self, h):
        self.h, self.n = h, 0


class Phase:
    def __init__(self):
        self.q = {e: [] for e in ENGS}

    def emit(self, eng, f):
        self.q[eng].append(f)

    def run(self, block):
        for name in ENGS:
            fns = self.q[name]
            if fns:
                getattr(block, name)(lambda E, fns=fns: [f(E) for f in fns])


def build_nc(debug=False, upto=9, ntb=NTB, nh=NH, ne=NE):
    nc = bass.Bass("TRN2", target_bir_lowering=False, debug=False)
    dt = lambda n, s, k="ExternalInput", d=F32: nc.dram_tensor(n, s, d, kind=k)

    x_full = dt("x_full", [S, HID]).ap()
    x_own = dt("x_own", [1024, HID]).ap()
    wq_img = dt("wq_img", [NH, 128, NKC, 128]).ap()
    wk_img = dt("wk_img", [NH, 128, NKC, 128]).ap()
    wv_img = dt("wv_img", [NH, 128, NKC, 128]).ap()
    wo_img = dt("wo_img", [4, 128, NKC, 512]).ap()
    bq_r = dt("bq_r", [128, NH]).ap()
    bk_r = dt("bk_r", [128, NH]).ap()
    bv_r = dt("bv_r", [128, NH]).ap()
    bo_bc = dt("bo_bc", [128, HID]).ap()
    wmap_q = dt("wmap_qT", [128, DIM]).ap()
    wmap_k = dt("wmap_kT", [128, DIM]).ap()
    gate_img = dt("gate_img", [128, NKC, NE]).ap()
    gu_img = dt("gu_img", [NE, 4, 128, NKC, 256]).ap()
    dp_img = dt("dp_img", [NE, 128, 4, HID]).ap()
    shg_img = dt("shg_img", [128, NKC, NI]).ap()
    shu_img = dt("shu_img", [128, NKC, NI]).ap()
    shd_img = dt("shd_img", [128, 4, HID]).ap()
    shg1_img = dt("shg1_img", [128, NKC, 1]).ap()
    ident_in = dt("ident_bf", [128, 128], d=BF16).ap()
    identf_in = dt("ident_f", [128, 128]).ap()
    out_t = dt("out", [1024, HID], k="ExternalOutput").ap()
    dbg = {}
    if debug:
        dbg["xn"] = dt("d_xn", [128, NKC, 1024], k="ExternalOutput", d=BF16).ap()
        dbg["kv"] = dt("d_kv", [64, NH, 128], k="ExternalOutput", d=BF16).ap()
        dbg["ksum"] = dt("d_ksum", [64, NH], k="ExternalOutput", d=BF16).ap()
        dbg["attn"] = dt("d_attn", [128, NKC, 1024], k="ExternalOutput", d=BF16).ap()
        dbg["accw"] = dt("d_accw", [128, NT, HID], k="ExternalOutput").ap()
        dbg["W"] = dt("d_W", [128, NT, NE], k="ExternalOutput").ap()
        dbg["shg"] = dt("d_shg", [128, NT], k="ExternalOutput").ap()

    est = contextlib.ExitStack()
    with est:
        sb = lambda n, s, d: est.enter_context(nc.sbuf_tensor(n, s, d))
        mksem = lambda n: Sem(est.enter_context(nc.semaphore(n)))

        xn_own = sb("xn_own", [128, NKC, 1024], BF16)
        kv_bf = sb("kv_bf", [64, NH, 128], BF16)
        ksum_bf = sb("ksum_bf", [64, NH], BF16)
        W_sb = sb("W_sb", [128, NT, NE], F32)
        shgate_sb = sb("shgate_sb", [128, NT], F32)
        gateT_bf = sb("gateT_bf", [128, NKC, NE], BF16)
        shg1_bf = sb("shg1_bf", [128, NKC, 1], BF16)
        wmq_bf = sb("wmq_bf", [128, DIM], BF16)
        wmk_bf = sb("wmk_bf", [128, DIM], BF16)
        bq_sb = sb("bq_sb", [128, NH], F32)
        bk_sb = sb("bk_sb", [128, NH], F32)
        bv_sb = sb("bv_sb", [128, NH], F32)
        ident = sb("ident", [128, 128], BF16)
        ones_bf = sb("ones_bf", [128, 1], BF16)
        iota_sb = sb("iota_sb", [128, NE], F32)
        lnsc = sb("lnsc", [128, 8], F32)
        rt_sc = sb("rt_sc", [128, 8], F32)
        rn_sc = sb("rn_sc", [128, NH + 2], F32)
        rn2_sc = sb("rn2_sc", [128, NH + 2], F32)

        s_x = mksem("s_x")
        s_w = mksem("s_w")
        s_xp = [mksem("s_xp0"), mksem("s_xp1")]
        s_wk = [mksem("s_wk0"), mksem("s_wk1"), mksem("s_wk2")]
        s_wv = [mksem("s_wv0"), mksem("s_wv1"), mksem("s_wv2")]
        s_wq = [mksem("s_wq0"), mksem("s_wq1"), mksem("s_wq2")]
        s_wo2 = [mksem("s_wo0"), mksem("s_wo1")]
        s_gu = [mksem("s_gu0"), mksem("s_gu1")]
        s_dp = [mksem("s_dp0"), mksem("s_dp1")]
        s_op = [mksem("s_op0"), mksem("s_op1")]
        s_dbg = mksem("s_dbg")
        s_t = mksem("s_t")
        s_v = mksem("s_v")
        s_s = mksem("s_s")
        s_o = mksem("s_o")
        s_ph = mksem("s_ph")

        block = est.enter_context(nc.Block())

        def dma(p, eng, sem, out, in_):
            p.emit(eng, lambda E, out=out, in_=in_: E.dma_start(out=out, in_=in_).then_inc(sem.h, 16))
            sem.n += 16
            return sem.n

        def wait(p, eng, sem, val=None):
            v = sem.n if val is None else val
            if v > 0:
                p.emit(eng, lambda E, v=v: E.wait_ge(sem.h, v))

        def inc(p, eng, sem, f):
            p.emit(eng, lambda E, f=f: f(E).then_inc(sem.h, 1))
            sem.n += 1
            return sem.n

        def op(p, eng, f):
            p.emit(eng, f)

        def barrier(p):
            for e in ENGS:
                p.emit(e, lambda E: E.nop().then_inc(s_ph.h, 1))
            s_ph.n += len(ENGS)
            for e in ENGS:
                wait(p, e, s_ph)

        def seq(p, eng, builders):
            sem = {"vector": s_v, "scalar": s_s, "tensor": s_t}[eng]
            last = 0
            for b in builders:
                last = inc(p, eng, sem, b)
                wait(p, eng, sem, last)
            return last

        def phase_sync(p):
            for e in ("tensor", "vector", "scalar"):
                for sem in (s_v, s_s, s_t):
                    wait(p, e, sem)

        def run(p):
            p.run(block)

        T = dict(trps_free=0, xn_free=0)

        def emit_ln(p, x_tile, xn_bf_out):
            ln_xsq = T["xsq"]
            ventry = s_v.n
            seq(p, "vector", [
                lambda E: E.reduce_sum(lnsc[:, 0:1], x_tile, axis=AX),
                lambda E: E.tensor_scalar(lnsc[:, 1:2], lnsc[:, 0:1], 1.0 / HID, None, ALU.mult),
            ])
            wait(p, "scalar", s_v, ventry)
            g_sq = inc(p, "scalar", s_s, lambda E: E.activation(ln_xsq[:], x_tile, AF.Square))
            wait(p, "vector", s_s, g_sq)
            g_rv = seq(p, "vector", [
                lambda E: E.reduce_sum(lnsc[:, 2:3], ln_xsq[:], axis=AX),
                lambda E: E.tensor_scalar(lnsc[:, 3:4], lnsc[:, 2:3], 1.0 / HID, None, ALU.mult),
                lambda E: E.tensor_mul(lnsc[:, 4:5], lnsc[:, 1:2], lnsc[:, 1:2]),
                lambda E: E.tensor_sub(lnsc[:, 5:6], lnsc[:, 3:4], lnsc[:, 4:5]),
                lambda E: E.tensor_scalar(lnsc[:, 5:6], lnsc[:, 5:6], EPS_LN, None, ALU.add),
                lambda E: E.reciprocal(lnsc[:, 6:7], lnsc[:, 5:6]),
            ])
            wait(p, "scalar", s_v, g_rv)
            g_st = inc(p, "scalar", s_s, lambda E: E.activation(lnsc[:, 7:8], lnsc[:, 6:7], AF.Sqrt))
            wait(p, "vector", s_s, g_st)
            gv = inc(p, "vector", s_v, lambda E: E.tensor_scalar(
                xn_bf_out, x_tile, lnsc[:, 1:2], lnsc[:, 7:8], ALU.subtract, ALU.mult))
            return gv, g_st

        def emit_t16(p, trps, src_bf, dst_col_fn, idn=None):
            if idn is None:
                idn = ident
            B = trps.shape[1] // 128
            nb = NKC // B
            for batch in range(nb):
                wait(p, "tensor", s_v, T["trps_free"])
                for j in range(B - 1):
                    kc = batch * B + j
                    op(p, "tensor", lambda E, kc=kc, j=j: E.transpose(
                        trps[:, j * 128:(j + 1) * 128], src_bf[:, kc * 128:(kc + 1) * 128], idn[:]))
                kc = batch * B + B - 1
                gt = inc(p, "tensor", s_t, lambda E, kc=kc, B=B: E.transpose(
                    trps[:, (B - 1) * 128:B * 128], src_bf[:, kc * 128:(kc + 1) * 128], idn[:]))
                wait(p, "vector", s_t, gt)
                dst = dst_col_fn(batch, B)
                T["trps_free"] = inc(p, "vector", s_v, lambda E, dst=dst, B=B: E.tensor_copy(
                    dst, trps[:, 0:B * 128].rearrange("p (a b) -> p a b", a=B)))

        p0 = Phase()
        dma(p0, "sync", s_x, bq_sb[:], bq_r)
        dma(p0, "sync", s_x, bk_sb[:], bk_r)
        dma(p0, "sync", s_x, bv_sb[:], bv_r)
        dma(p0, "sync", s_x, ident[:], ident_in)
        dma(p0, "gpsimd", s_w, wmq_bf[:], wmap_q)
        dma(p0, "gpsimd", s_w, wmk_bf[:], wmap_k)
        dma(p0, "gpsimd", s_w, gateT_bf[:], gate_img)
        dma(p0, "gpsimd", s_w, shg1_bf[:], shg1_img)
        op(p0, "gpsimd", lambda E: E.iota(iota_sb[:], pattern=[[1, NE]], base=0,
                                          channel_multiplier=0,
                                          allow_small_or_imprecise_dtypes=True))
        op(p0, "vector", lambda E: E.memset(ones_bf[:], 1.0))
        for e in ("vector", "scalar", "tensor"):
            wait(p0, e, s_x)
            wait(p0, e, s_w)
        barrier(p0)
        run(p0)

        if upto >= 1:
          with contextlib.ExitStack() as ph:
            psb = lambda n, s, d: ph.enter_context(nc.psum_tensor(n, s, d))
            lsb = lambda n, s, d: ph.enter_context(nc.sbuf_tensor(n, s, d))
            kv_ps = psb("kv_ps", [64, NH, 128], F32)
            trps = psb("trps", [128, 1024], BF16)
            small2 = psb("small2", [128, 512], F32)
            k_ps = psb("k_ps", [128, 512], F32)
            v_ps = psb("v_ps", [128, 512], F32)
            k2_ps = small2[:, 0:256]
            kst = small2[0:64, 256:257]

            x_t2 = lsb("x_t2", [128, 2, HID], F32)
            xn_cast = lsb("xn_cast", [128, 2, HID], BF16)
            v_sb = lsb("v_sb", [128, 2, 512], BF16)
            ksum_acc = lsb("ksum_acc", [64, NH], F32)
            T["xsq"] = lsb("ln_xsq1", [128, HID], F32)
            xn_tb = lsb("xn_tb", [128, 2, NKC, 512], BF16)
            wk_sl = lsb("wk_sl", [128, 3, NKC, 128], BF16)
            wv_sl = lsb("wv_sl", [128, 3, NKC, 128], BF16)
            k_sb = lsb("k_sb", [128, 2, 512], BF16)
            v_tm = lsb("v_tm", [128, 2, 4, 128], BF16)
            k2_sb = lsb("k2_sb", [128, 2, 4, DIM], BF16)
            et = lsb("et", [128, 3, 256], F32)

            p1 = Phase()
            phase_sync(p1)
            kv_started = [False] * 4
            xt_free = [(0, 0), (0, 0)]
            wk_free = [0, 0, 0]
            wv_free = [0, 0, 0]
            XN_READY = [0, 0]
            XN_FREE = [0, 0]
            xchunk = [0]
            H = {}

            XLD = {}
            XLN = {}
            xc_free = [0, 0]
            lnchunk = [0]

            def emit_x_load(tb, ch, own=False):
                bi = xchunk[0] % 2
                xchunk[0] += 1
                fv, fs = xt_free[bi]
                wait(p1, "sync", s_v, fv)
                wait(p1, "sync", s_s, fs)
                r0 = ch * 128 if own else tb * 512 + ch * 128
                src = x_own if own else x_full
                XLD[(tb, ch, own)] = (bi, dma(p1, "sync", s_xp[bi], x_t2[:, bi, :], src[r0:r0 + 128, :]))

            def emit_ln_part(tb, ch, own=False):
                bi, ld = XLD.pop((tb, ch, own))
                cb = lnchunk[0] % 2
                lnchunk[0] += 1
                wait(p1, "vector", s_xp[bi], ld)
                wait(p1, "scalar", s_xp[bi], ld)
                wait(p1, "vector", s_t, xc_free[cb])
                gv, gs = emit_ln(p1, x_t2[:, bi, :], xn_cast[:, cb, :])
                xt_free[bi] = (gv, gs)
                XLN[(tb, ch, own)] = (cb, gv)

            def emit_trps_part(tb, ch, own=False):
                par = tb % 2
                cb, gv = XLN.pop((tb, ch, own))
                if ch == 0 and not own:
                    wait(p1, "vector", s_t, XN_FREE[par])
                wait(p1, "tensor", s_v, gv)
                if own:
                    dst = lambda batch, B, ch=ch: xn_own[:, batch * B:(batch + 1) * B,
                                                        ch * 128:(ch + 1) * 128]
                else:
                    dst = lambda batch, B, ch=ch, par=par: xn_tb[:, par, batch * B:(batch + 1) * B,
                                                                 ch * 128:(ch + 1) * 128]
                emit_t16(p1, trps[:, 0:1024] if False else trps, xn_cast[:, cb, :], dst)
                xc_free[cb] = s_t.n

            def emit_stageA_chunk(tb, ch):
                emit_x_load(tb, ch)
                emit_ln_part(tb, ch)
                emit_trps_part(tb, ch)

            def emit_ldkv(key, h, gi):
                bi = gi % 3
                d = H.setdefault(key, {})
                d["bi"] = bi
                wait(p1, "gpsimd", s_t, wk_free[bi])
                d["ldk"] = dma(p1, "gpsimd", s_wk[bi], wk_sl[:, bi], wk_img[h])
                wait(p1, "gpsimd", s_t, wv_free[bi])
                d["ldv"] = dma(p1, "gpsimd", s_wv[bi], wv_sl[:, bi], wv_img[h])

            def emit_kvproj(key, prev):
                tb, h = key
                hp = h % 2
                par = tb % 2
                d = H[key]
                bi = d["bi"]
                wait(p1, "tensor", s_wk[bi], d["ldk"])
                wait(p1, "tensor", s_v, XN_READY[par])
                if prev is not None:
                    wait(p1, "tensor", s_s, H[prev]["ksb"])
                for kc in range(NKC - 1):
                    op(p1, "tensor", lambda E, kc=kc, bi=bi, par=par: E.matmul(
                        k_ps[:], wk_sl[:, bi, kc, :], xn_tb[:, par, kc, :],
                        start=(kc == 0), stop=False))
                wk_free[bi] = inc(p1, "tensor", s_t, lambda E, bi=bi, par=par: E.matmul(
                    k_ps[:], wk_sl[:, bi, NKC - 1, :], xn_tb[:, par, NKC - 1, :],
                    start=False, stop=True))
                d["kps"] = s_t.n
                wait(p1, "scalar", s_t, d["kps"])
                d["ksb"] = inc(p1, "scalar", s_s, lambda E, h=h, hp=hp: E.activation(
                    k_sb[:, hp, :], k_ps[:], AF.Identity, bias=bk_sb[:, h:h + 1]))
                wait(p1, "tensor", s_wv[bi], d["ldv"])
                if prev is not None:
                    wait(p1, "tensor", s_s, H[prev]["vsb"])
                for kc in range(NKC - 1):
                    op(p1, "tensor", lambda E, kc=kc, bi=bi, par=par: E.matmul(
                        v_ps[:], wv_sl[:, bi, kc, :], xn_tb[:, par, kc, :],
                        start=(kc == 0), stop=False))
                wv_free[bi] = inc(p1, "tensor", s_t, lambda E, bi=bi, par=par: E.matmul(
                    v_ps[:], wv_sl[:, bi, NKC - 1, :], xn_tb[:, par, NKC - 1, :],
                    start=False, stop=True))
                d["vps"] = s_t.n
                wait(p1, "scalar", s_t, d["vps"])
                d["vsb"] = inc(p1, "scalar", s_s, lambda E, h=h, hp=hp: E.activation(
                    v_sb[:, hp, :], v_ps[:], AF.Identity, bias=bv_sb[:, h:h + 1]))

            def emit_k2elu(key, prev):
                tb, h = key
                hp = h % 2
                d = H[key]
                wait(p1, "tensor", s_s, d["ksb"])
                pv = max(H[prev]["k2sb"], H[prev]["vtm"], H[prev]["ksadd"]) if prev is not None else 0
                wait(p1, "tensor", s_v, pv)
                for sc4 in range(3):
                    op(p1, "tensor", lambda E, sc4=sc4, st=(sc4 == 0), hp=hp: E.matmul(
                        k2_ps[:, sc4 * 64:(sc4 + 1) * 64],
                        k_sb[:, hp, sc4 * 128:(sc4 + 1) * 128], wmk_bf[:],
                        start=st, stop=True, skip_group_check=True))
                g_k2 = inc(p1, "tensor", s_t, lambda E, hp=hp: E.matmul(
                    k2_ps[:, 3 * 64:4 * 64],
                    k_sb[:, hp, 3 * 128:4 * 128], wmk_bf[:],
                    start=False, stop=True, skip_group_check=True))
                wait(p1, "scalar", s_t, g_k2)
                gs1 = inc(p1, "scalar", s_s, lambda E: E.activation(et[:, 1, :], k2_ps[:], AF.Exp))
                wait(p1, "vector", s_t, g_k2)
                wait(p1, "vector", s_s, gs1)
                d["elu_rd"] = seq(p1, "vector", [lambda E: E.tensor_scalar(
                    et[:, 2, :], k2_ps[:], 0.0, None, ALU.max)])
                d["k2sb"] = inc(p1, "vector", s_v, lambda E, hp=hp: E.scalar_tensor_tensor(
                    k2_sb[:, hp, :, :].rearrange("p a b -> p (a b)"), et[:, 1, :], 1.0,
                    et[:, 2, :], ALU.min, ALU.add))

            def emit_vt(key):
                tb, h = key
                hp = h % 2
                d = H[key]
                wait(p1, "tensor", s_s, d["vsb"])
                wait(p1, "tensor", s_v, T["trps_free"])
                for j in range(3):
                    op(p1, "tensor", lambda E, j=j, hp=hp: E.transpose(
                        trps[:, j * 128:(j + 1) * 128], v_sb[:, hp, j * 128:(j + 1) * 128], ident[:]))
                g_vt = inc(p1, "tensor", s_t, lambda E, hp=hp: E.transpose(
                    trps[:, 3 * 128:4 * 128], v_sb[:, hp, 3 * 128:4 * 128], ident[:]))
                wait(p1, "vector", s_t, g_vt)
                d["vtm"] = inc(p1, "vector", s_v, lambda E, hp=hp: E.tensor_copy(
                    v_tm[:, hp, :, :].rearrange("p a b -> p (a b)"), trps[:, 0:512]))
                T["trps_free"] = d["vtm"]

            def emit_kv(key):
                tb, h = key
                hp = h % 2
                d = H[key]
                bank = h // 4
                wait(p1, "tensor", s_v, max(d["k2sb"], d["vtm"]))
                for sc4 in range(4):
                    op(p1, "tensor", lambda E, h=h, sc4=sc4, st=not kv_started[bank], hp=hp: E.matmul(
                        kv_ps[:, h, :], k2_sb[:, hp, sc4, :], v_tm[:, hp, sc4, :],
                        start=st, stop=True, skip_group_check=True))
                    kv_started[bank] = True
                    if sc4 < 3:
                        op(p1, "tensor", lambda E, sc4=sc4, st=(sc4 == 0), hp=hp: E.matmul(
                            kst[:], k2_sb[:, hp, sc4, :], ones_bf[:],
                            start=st, stop=True, skip_group_check=True))
                g_ks = inc(p1, "tensor", s_t, lambda E, hp=hp: E.matmul(
                    kst[:], k2_sb[:, hp, 3, :], ones_bf[:],
                    start=False, stop=True, skip_group_check=True))
                wait(p1, "vector", s_t, g_ks)
                d["ksadd"] = seq(p1, "vector", [lambda E, h=h: E.tensor_add(
                    ksum_acc[:, h:h + 1], ksum_acc[:, h:h + 1], kst[:])])

            seq(p1, "vector", [lambda E: E.memset(ksum_acc[:], 0.0)])
            for ch in range(4):
                emit_stageA_chunk(0, ch)
            XN_READY[0] = s_v.n

            own_done = set()
            heads = [(tb, h) for tb in range(ntb) for h in range(nh)]
            emit_ldkv(heads[0], heads[0][1], 0)
            if len(heads) > 1:
                emit_ldkv(heads[1], heads[1][1], 1)
            emit_kvproj(heads[0], None)
            for i, key in enumerate(heads):
                tb, h = key
                prev = heads[i - 1] if i > 0 else None
                full = (nh == NH)
                if full:
                    if tb + 1 < ntb and h in (6, 7, 9, 11):
                        emit_x_load(tb + 1, {6: 0, 7: 1, 9: 2, 11: 3}[h])
                    if tb in (1, 2) and h in (0, 2, 4, 6):
                        own_done.add((tb - 1) * 4 + h // 2)
                        emit_x_load(tb, (tb - 1) * 4 + h // 2, own=True)
                if i + 2 < len(heads):
                    emit_ldkv(heads[i + 2], heads[i + 2][1], i + 2)
                emit_k2elu(key, prev)
                emit_vt(key)
                if full:
                    if tb + 1 < ntb and h in (8, 10, 12, 14):
                        emit_ln_part(tb + 1, (h - 8) // 2)
                    if tb in (1, 2) and h in (1, 3, 5, 7):
                        emit_ln_part(tb, (tb - 1) * 4 + (h - 1) // 2, own=True)
                if i + 1 < len(heads):
                    emit_kvproj(heads[i + 1], key)
                emit_kv(key)
                if full:
                    if tb + 1 < ntb and h in (8, 10, 12, 14):
                        emit_trps_part(tb + 1, (h - 8) // 2)
                        if h == 14:
                            XN_READY[(tb + 1) % 2] = s_v.n
                    if tb in (1, 2) and h in (1, 3, 5, 7):
                        emit_trps_part(tb, (tb - 1) * 4 + (h - 1) // 2, own=True)
                if h == nh - 1:
                    XN_FREE[tb % 2] = H[key]["vps"]
                if nh < NH and h == nh - 1 and tb + 1 < ntb:
                    for ch in range(4):
                        emit_stageA_chunk(tb + 1, ch)
                    XN_READY[(tb + 1) % 2] = s_v.n
            for ch in range(NT):
                if ch not in own_done:
                    emit_x_load(0, ch, own=True)
                    emit_ln_part(0, ch, own=True)
                    emit_trps_part(0, ch, own=True)

            wait(p1, "vector", s_t, s_t.n)
            inc(p1, "vector", s_v, lambda E: E.tensor_copy(
                kv_bf[:, 0:nh, :].rearrange("p a b -> p (a b)"),
                kv_ps[:, 0:nh, :].rearrange("p a b -> p (a b)")))
            inc(p1, "vector", s_v, lambda E: E.tensor_copy(ksum_bf[:, 0:nh], ksum_acc[:, 0:nh]))
            if debug:
                wait(p1, "sync", s_v, s_v.n)
                dma(p1, "sync", s_dbg, dbg["kv"][:, 0:nh, :].rearrange("p a b -> p (a b)") if nh < NH else dbg["kv"].rearrange("p a b -> p (a b)"), kv_bf[:, 0:nh, :].rearrange("p a b -> p (a b)"))
                dma(p1, "sync", s_dbg, dbg["ksum"][:, 0:nh] if nh < NH else dbg["ksum"], ksum_bf[:, 0:nh])
                wait(p1, "sync", s_dbg)
            barrier(p1)
            run(p1)
        T["trps_free"] = 0
        T["xn_free"] = 0

        acc = sb("acc", [128, NT, HID], F32)
        if upto >= 2:
          with contextlib.ExitStack() as ph:
            lsb = lambda n, s, d: ph.enter_context(nc.sbuf_tensor(n, s, d))
            attn_fm = lsb("attn_fm", [128, NKC, 1024], BF16)

            if debug:
                pxa = Phase()
                dma(pxa, "sync", s_dbg, dbg["xn"], xn_own[:])
                wait(pxa, "sync", s_dbg)
                run(pxa)

            with (
                nc.psum_tensor("out_ps", [128, HID], F32) as out_ps,
                nc.psum_tensor("small3", [128, 512], F32) as small3,
                nc.psum_tensor("trps3", [128, 1024], BF16) as trps,
            ):
                nm_ps = small3[:, 0:16]
                q2h = ph.enter_context(nc.sbuf_tensor("q2h", [64, 2, NH, 512], BF16))
                attn_tm = ph.enter_context(nc.sbuf_tensor("attn_tm", [128, HID], BF16))
                RQ = []
                rq_i = [0]
                SEMS = {"tensor": s_t, "vector": s_v, "scalar": s_s}

                def queue_routing(t, src, bufs):
                    rwork, rsel, rtmp, rex = bufs
                    RQ.append(("vector", [lambda E: E.tensor_copy(rwork[:], src)], (), False))
                    RQ.append(("vector", [lambda E: E.memset(rsel[:], 0.0)], (), False))
                    RQ.append(("vector", [lambda E: E.reduce_max(
                        rt_sc[:, 1:2], src, axis=AX, negate=True)], (), False))
                    for j in range(TOPK):
                        for b, wp in [
                            (lambda E: E.reduce_max(rt_sc[:, 0:1], rwork[:], axis=AX), False),
                            (lambda E: E.tensor_scalar(rtmp[:], rwork[:], rt_sc[:, 0:1], None, ALU.is_equal), True),
                            (lambda E: E.tensor_scalar(rtmp[:], rtmp[:], -BIG, BIG, ALU.mult, ALU.add), False),
                            (lambda E: E.tensor_add(rtmp[:], rtmp[:], iota_sb[:]), False),
                            (lambda E: E.tensor_reduce(rt_sc[:, 2:3], rtmp[:], AX, ALU.min), False),
                            (lambda E: E.tensor_scalar(rtmp[:], iota_sb[:], rt_sc[:, 2:3], None, ALU.is_equal), True),
                            (lambda E: E.tensor_add(rsel[:], rsel[:], rtmp[:]), False),
                            (lambda E: E.scalar_tensor_tensor(rwork[:], rtmp[:], -BIG, rwork[:], ALU.mult, ALU.add), False),
                        ]:
                            RQ.append(("vector", [b], (), wp))
                    RQ.append(("scalar", [lambda E: E.activation(
                        rex[:], src, AF.Exp, bias=rt_sc[:, 1:2])], ("vector",), False))
                    RQ.append(("vector", [lambda E: E.tensor_mul(rex[:], rex[:], rsel[:])], ("scalar",), False))
                    RQ.append(("vector", [lambda E: E.reduce_sum(rt_sc[:, 3:4], rex[:], axis=AX)], (), False))
                    RQ.append(("vector", [lambda E: E.reciprocal(rt_sc[:, 4:5], rt_sc[:, 3:4])], (), True))
                    RQ.append(("vector", [lambda E, t=t: E.tensor_scalar(
                        W_sb[:, t, :], rex[:], rt_sc[:, 4:5], None, ALU.mult)], (), True))

                def drip(p, n):
                    end = min(rq_i[0] + n, len(RQ))
                    while rq_i[0] < end:
                        eng, builders, wengs, wprev = RQ[rq_i[0]]
                        if wprev:
                            wait(p, eng, SEMS[eng])
                        for we in wengs:
                            wait(p, eng, SEMS[we])
                        for b in builders[:-1]:
                            op(p, eng, b)
                        inc(p, eng, SEMS[eng], builders[-1])
                        rq_i[0] += 1

                atn_ready = [0] * 8
                C2 = dict(rn_rd=0, evac=0)
                AT = {}

                def attn_p1(p, half, t):
                    wait(p, "tensor", s_v, C2["rn_rd"])
                    for h in range(NH - 1):
                        op(p, "tensor", lambda E, h=h, t=t, half=half: E.matmul(
                            nm_ps[:, h:h + 1], q2h[:, half, h, t * 128:(t + 1) * 128],
                            ksum_bf[:, h:h + 1], start=(h == 0), stop=True,
                            skip_group_check=True))
                    g_nm = inc(p, "tensor", s_t, lambda E, t=t, half=half: E.matmul(
                        nm_ps[:, NH - 1:NH], q2h[:, half, NH - 1, t * 128:(t + 1) * 128],
                        ksum_bf[:, NH - 1:NH], start=False, stop=True,
                        skip_group_check=True))
                    wait(p, "vector", s_t, g_nm)
                    rn = AT[(half, t)]["rn"]
                    g_rn = seq(p, "vector", [
                        (lambda rn: lambda E: E.tensor_scalar(
                            rn, nm_ps[:, 0:NH], EPS_AT, None, ALU.add))(rn),
                        (lambda rn: lambda E: E.reciprocal(rn, rn))(rn),
                    ])
                    C2["rn_rd"] = g_rn
                    AT[(half, t)]["g_rn"] = g_rn

                def attn_p2(p, half, t):
                    wait(p, "tensor", s_v, C2["evac"])
                    for h in range(NH - 1):
                        op(p, "tensor", lambda E, h=h, t=t, half=half: E.matmul(
                            out_ps[:, h * 128:(h + 1) * 128],
                            q2h[:, half, h, t * 128:(t + 1) * 128], kv_bf[:, h, :],
                            start=(h % 4 == 0), stop=True, skip_group_check=True))
                    g_out = inc(p, "tensor", s_t, lambda E, t=t, half=half: E.matmul(
                        out_ps[:, (NH - 1) * 128:NH * 128],
                        q2h[:, half, NH - 1, t * 128:(t + 1) * 128], kv_bf[:, NH - 1, :],
                        start=False, stop=True, skip_group_check=True))
                    d = AT[(half, t)]
                    wait(p, "vector", s_t, g_out)
                    wait(p, "vector", s_v, d["g_rn"])
                    rn = d["rn"]
                    for h in range(NH - 1):
                        op(p, "vector", lambda E, h=h, rn=rn: E.tensor_scalar(
                            attn_tm[:, h * 128:(h + 1) * 128],
                            out_ps[:, h * 128:(h + 1) * 128],
                            rn[:, h:h + 1], None, ALU.mult))
                    g_ev = inc(p, "vector", s_v, lambda E, rn=rn: E.tensor_scalar(
                        attn_tm[:, (NH - 1) * 128:NH * 128],
                        out_ps[:, (NH - 1) * 128:NH * 128],
                        rn[:, NH - 1:NH], None, ALU.mult))
                    C2["evac"] = g_ev
                    d["g_ev"] = g_ev

                def attn_p3(p, half, t):
                    ti = half * 4 + t
                    d = AT.pop((half, t))
                    wait(p, "tensor", s_v, d["g_ev"])
                    emit_t16(p, trps, attn_tm,
                             lambda batch, B, ti=ti: attn_fm[:, batch * B:(batch + 1) * B,
                                                            ti * 128:(ti + 1) * 128])
                    atn_ready[ti] = s_v.n

                with contextlib.ExitStack() as qs:
                    wq_sl = qs.enter_context(nc.sbuf_tensor("wq_sl", [128, 3, NKC, 128], BF16))
                    qf_sb = qs.enter_context(nc.sbuf_tensor("qf_sb", [128, 2, 512], BF16))
                    etq = qs.enter_context(nc.sbuf_tensor("etq", [128, 2, 512], F32))
                    lg_sb4 = qs.enter_context(nc.sbuf_tensor("lg_sb4", [128, 4, NE], F32))
                    rbufs_q = tuple(qs.enter_context(nc.sbuf_tensor(n, [128, NE], F32))
                                    for n in ("rwork_q", "rsel_q", "rtmp_q", "rex_q"))
                    q_ps = qs.enter_context(nc.psum_tensor("q_ps", [128, 512], F32))
                    q2_ps = qs.enter_context(nc.psum_tensor("q2_ps", [64, 512], F32))

                    for half in range(2):
                        p2b = Phase()
                        phase_sync(p2b)
                        if half == 0 and upto >= 3:
                            g_lg4 = 0
                            for t in range(4):
                                bs = t * 128
                                for kc in range(NKC):
                                    op(p2b, "tensor", lambda E, kc=kc, t=t, bs=bs: E.matmul(
                                        out_ps[:, bs:bs + NE],
                                        xn_own[:, kc, t * 128:(t + 1) * 128],
                                        gateT_bf[:, kc, :],
                                        start=(t == 0 and kc == 0),
                                        stop=(kc == NKC - 1), skip_group_check=True))
                                for kc in range(NKC):
                                    bld = lambda E, kc=kc, t=t, bs=bs: E.matmul(
                                        out_ps[:, bs + 64:bs + 65],
                                        xn_own[:, kc, t * 128:(t + 1) * 128],
                                        shg1_bf[:, kc, :], start=False,
                                        stop=(kc == NKC - 1), skip_group_check=True)
                                    if t == 3 and kc == NKC - 1:
                                        g_lg4 = inc(p2b, "tensor", s_t, bld)
                                    else:
                                        op(p2b, "tensor", bld)
                            wait(p2b, "vector", s_t, g_lg4)
                            for t in range(4):
                                bld = lambda E, t=t: E.tensor_copy(
                                    lg_sb4[:, t, :], out_ps[:, t * 128:t * 128 + NE])
                                if t == 3:
                                    inc(p2b, "vector", s_v, bld)
                                else:
                                    op(p2b, "vector", bld)
                            wait(p2b, "scalar", s_t, g_lg4)
                            for t in range(4):
                                op(p2b, "scalar", lambda E, t=t: E.activation(
                                    shgate_sb[:, t:t + 1],
                                    out_ps[:, t * 128 + 64:t * 128 + 65], AF.Sigmoid))
                            for t in range(4):
                                queue_routing(t, lg_sb4[:, t, :], rbufs_q)
                        wq_free = [0, 0, 0]
                        QH = {}

                        def q_ldkv(h):
                            bi = h % 3
                            wait(p2b, "gpsimd", s_t, wq_free[bi])
                            QH.setdefault(h, {})["ldq"] = dma(
                                p2b, "gpsimd", s_wq[bi], wq_sl[:, bi], wq_img[h])

                        def q_proj(h):
                            bi = h % 3
                            d = QH[h]
                            wait(p2b, "tensor", s_wq[bi], d["ldq"])
                            if h >= 1:
                                wait(p2b, "tensor", s_s, QH[h - 1]["qf"])
                            for kc in range(NKC - 1):
                                op(p2b, "tensor", lambda E, kc=kc, bi=bi, half=half: E.matmul(
                                    q_ps[:], wq_sl[:, bi, kc, :],
                                    xn_own[:, kc, half * 512:(half + 1) * 512],
                                    start=(kc == 0), stop=False))
                            wq_free[bi] = inc(p2b, "tensor", s_t, lambda E, bi=bi, half=half: E.matmul(
                                q_ps[:], wq_sl[:, bi, NKC - 1, :],
                                xn_own[:, NKC - 1, half * 512:(half + 1) * 512],
                                start=False, stop=True))
                            wait(p2b, "scalar", s_t, s_t.n)
                            d["qf"] = inc(p2b, "scalar", s_s, lambda E, h=h, qb=h % 2: E.activation(
                                qf_sb[:, qb, :], q_ps[:], AF.Identity, bias=bq_sb[:, h:h + 1]))

                        def q_map(h):
                            d = QH[h]
                            wait(p2b, "tensor", s_s, d["qf"])
                            if h >= 1:
                                wait(p2b, "tensor", s_v, QH[h - 1]["q2sb"])
                            g_q2 = inc(p2b, "tensor", s_t, lambda E, qb=h % 2: E.matmul(
                                q2_ps[:], wmq_bf[:], qf_sb[:, qb, :], start=True, stop=True))
                            wait(p2b, "scalar", s_t, g_q2)
                            gs1 = inc(p2b, "scalar", s_s, lambda E: E.activation(
                                etq[0:64, 1, :], q2_ps[:], AF.Exp))
                            wait(p2b, "vector", s_t, g_q2)
                            wait(p2b, "vector", s_s, gs1)
                            seq(p2b, "vector", [lambda E: E.tensor_scalar(
                                etq[0:64, 0, :], q2_ps[:], 0.0, None, ALU.max)])
                            d["q2sb"] = inc(p2b, "vector", s_v, lambda E, h=h, half=half: E.scalar_tensor_tensor(
                                q2h[:, half, h, :], etq[0:64, 1, :], 1.0,
                                etq[0:64, 0, :], ALU.min, ALU.add))

                        q_ldkv(0)
                        q_ldkv(1)
                        q_proj(0)
                        for h in range(NH):
                            if h + 2 < NH:
                                q_ldkv(h + 2)
                            if h + 1 < NH:
                                q_proj(h + 1)
                            q_map(h)
                            if upto >= 3:
                                drip(p2b, 9)
                            if half == 1:
                                t4, ph4 = h // 4, h % 4
                                if ph4 == 1:
                                    AT[(0, t4)] = {"rn": rn_sc[:, 0:NH] if t4 % 2 == 0 else rn2_sc[:, 0:NH]}
                                    attn_p1(p2b, 0, t4)
                                elif ph4 == 2:
                                    attn_p2(p2b, 0, t4)
                                elif ph4 == 3:
                                    attn_p3(p2b, 0, t4)
                        if upto >= 3 and half == 1:
                            drip(p2b, len(RQ))
                        barrier(p2b)
                        run(p2b)

                with contextlib.ExitStack() as hh:
                    woq = hh.enter_context(nc.sbuf_tensor("woq", [128, 2, NKC, 512], BF16))
                    lg_sb = hh.enter_context(nc.sbuf_tensor("lg_sb", [128, NE], F32))
                    rbufs_h = tuple(hh.enter_context(nc.sbuf_tensor(n, [128, NE], F32))
                                    for n in ("rwork_h", "rsel_h", "rtmp_h", "rex_h"))
                    lg_ps = small3[:, 128:256]
                    with nc.psum_tensor("wo_ps", [128, 2, 512], F32) as wo_ps:
                        p2d = Phase()
                        phase_sync(p2d)
                        wo_ld = [0, 0, 0, 0]
                        wo_ld[0] = dma(p2d, "gpsimd", s_wo2[0], woq[:, 0], wo_img[0])
                        wo_ld[1] = dma(p2d, "gpsimd", s_wo2[1], woq[:, 1], wo_img[1])
                        if upto >= 3:
                            for t in range(4, NT):
                                mms = []
                                for kc in range(NKC):
                                    mms.append(lambda E, kc=kc, t=t: E.matmul(
                                        lg_ps[:, 0:NE], xn_own[:, kc, t * 128:(t + 1) * 128],
                                        gateT_bf[:, kc, :], start=(kc == 0),
                                        stop=(kc == NKC - 1), skip_group_check=True))
                                for kc in range(NKC):
                                    mms.append(lambda E, kc=kc, t=t: E.matmul(
                                        lg_ps[:, 64:65], xn_own[:, kc, t * 128:(t + 1) * 128],
                                        shg1_bf[:, kc, :], start=False,
                                        stop=(kc == NKC - 1), skip_group_check=True))
                                RQ.append(("tensor", mms, ("vector", "scalar"), False))
                                RQ.append(("vector", [lambda E: E.tensor_copy(
                                    lg_sb[:], lg_ps[:, 0:NE])], ("tensor",), False))
                                RQ.append(("scalar", [lambda E, t=t: E.activation(
                                    shgate_sb[:, t:t + 1], lg_ps[:, 64:65],
                                    AF.Sigmoid)], ("tensor",), False))
                                queue_routing(t, lg_sb[:], rbufs_h)
                        wo_free = [0, 0]
                        ps_free = [0, 0]
                        at_stage = [0]

                        def maybe_attn():
                            st = at_stage[0]
                            if st >= 12:
                                return
                            t4, phs = st // 3, st % 3
                            if phs == 0:
                                AT[(1, t4)] = {"rn": rn_sc[:, 0:NH] if t4 % 2 == 0 else rn2_sc[:, 0:NH]}
                                attn_p1(p2d, 1, t4)
                            elif phs == 1:
                                attn_p2(p2d, 1, t4)
                            else:
                                attn_p3(p2d, 1, t4)
                            at_stage[0] += 1

                        for qt in range(4):
                            bi = qt % 2
                            if qt >= 2:
                                wait(p2d, "gpsimd", s_t, wo_free[bi])
                                wo_ld[qt] = dma(p2d, "gpsimd", s_wo2[bi], woq[:, bi], wo_img[qt])
                            wait(p2d, "tensor", s_wo2[bi], wo_ld[qt])
                            for t in range(NT):
                                if upto >= 3 and qt >= 1:
                                    drip(p2d, 13)
                                if qt == 0:
                                    maybe_attn()
                                    maybe_attn()
                                if t >= 4:
                                    wait(p2d, "tensor", s_v, atn_ready[t])
                                pp = (qt * NT + t) % 2
                                wait(p2d, "tensor", s_v, ps_free[pp])
                                for kc in range(NKC - 1):
                                    op(p2d, "tensor", lambda E, kc=kc, t=t, bi=bi, pp=pp: E.matmul(
                                        wo_ps[:, pp, :], attn_fm[:, kc, t * 128:(t + 1) * 128],
                                        woq[:, bi, kc, :], start=(kc == 0), stop=False))
                                g_mm = inc(p2d, "tensor", s_t, lambda E, t=t, bi=bi, pp=pp: E.matmul(
                                    wo_ps[:, pp, :], attn_fm[:, NKC - 1, t * 128:(t + 1) * 128],
                                    woq[:, bi, NKC - 1, :], start=False, stop=True))
                                if t == NT - 1:
                                    wo_free[bi] = g_mm
                                wait(p2d, "vector", s_t, g_mm)
                                ps_free[pp] = inc(p2d, "vector", s_v, lambda E, t=t, qt=qt, pp=pp: E.tensor_copy(
                                    acc[:, t, qt * 512:(qt + 1) * 512], wo_ps[:, pp, :]))
                        if upto >= 3:
                            drip(p2d, len(RQ))
                        if debug:
                            wait(p2d, "sync", s_v, s_v.n)
                            wait(p2d, "sync", s_s, s_s.n)
                            dma(p2d, "sync", s_dbg, dbg["attn"], attn_fm[:])
                            dma(p2d, "sync", s_dbg, dbg["accw"], acc[:])
                            dma(p2d, "sync", s_dbg, dbg["W"], W_sb[:])
                            dma(p2d, "sync", s_dbg, dbg["shg"], shgate_sb[:])
                            wait(p2d, "sync", s_dbg)
                        barrier(p2d)
                        run(p2d)


        sh_state = {}
        if upto >= 4:
          sh_outer = contextlib.ExitStack()
          est.enter_context(sh_outer)
          shg_bf = sh_outer.enter_context(nc.sbuf_tensor("shg_bf", [128, NKC, NI], BF16))
          shu_bf = sh_outer.enter_context(nc.sbuf_tensor("shu_bf", [128, NKC, NI], BF16))
          with contextlib.ExitStack() as ph:
            lsb = lambda n, s, d: ph.enter_context(nc.sbuf_tensor(n, s, d))
            gu_sl = lsb("gu_sl", [128, 2, NKC, 256], BF16)
            dp_sl = lsb("dp_sl", [128, 2, 4, HID], BF16)
            inter = lsb("inter", [128, 4, 1024], BF16)
            sil_t = lsb("sil_t", [128, 1024], F32)
            with (
                nc.psum_tensor("gu_ps", [128, 2, 2, 512], F32) as gu_ps,
                nc.psum_tensor("dn_ps", [128, 2, 1024], F32) as dn_ps,
            ):
                p3b = Phase()
                phase_sync(p3b)
                gu_free = [0, 0]
                dp_free = [0, 0]
                dn_free = [0, 0]
                slot_sil = [0, 0]
                slot_int = [0, 0]
                g_int_prev = [0, 0, 0, 0]
                for e in range(ne):
                    dbi = e % 2
                    if e == max(ne - 6, 0):
                        sh_state["l1"] = dma(p3b, "gpsimd", s_w, shg_bf[:], shg_img)
                        sh_state["l2"] = dma(p3b, "gpsimd", s_w, shu_bf[:], shu_img)
                    wait(p3b, "gpsimd", s_t, dp_free[dbi])
                    ldd = dma(p3b, "gpsimd", s_dp[dbi], dp_sl[:, dbi], dp_img[e])
                    for k in range(4):
                        qbi = (e * 4 + k) % 2
                        wait(p3b, "gpsimd", s_t, gu_free[qbi])
                        ldg = dma(p3b, "gpsimd", s_gu[qbi], gu_sl[:, qbi], gu_img[e, k])
                        wait(p3b, "tensor", s_gu[qbi], ldg)
                        wait(p3b, "tensor", s_v, g_int_prev[k])
                        g_int = 0
                        for th in range(2):
                            slot = (k * 2 + th) % 2
                            wait(p3b, "tensor", s_s, slot_sil[slot])
                            wait(p3b, "tensor", s_v, slot_int[slot])
                            for kc in range(NKC):
                                op(p3b, "tensor", lambda E, kc=kc, th=th, qbi=qbi, slot=slot: E.matmul(
                                    gu_ps[:, slot, 0, :],
                                    gu_sl[:, qbi, kc, 0:128],
                                    xn_own[:, kc, th * 512:(th + 1) * 512],
                                    start=(kc == 0), stop=(kc == NKC - 1)))
                                if not (kc == NKC - 1):
                                    op(p3b, "tensor", lambda E, kc=kc, th=th, qbi=qbi, slot=slot: E.matmul(
                                        gu_ps[:, slot, 1, :],
                                        gu_sl[:, qbi, kc, 128:256],
                                        xn_own[:, kc, th * 512:(th + 1) * 512],
                                        start=(kc == 0), stop=(kc == NKC - 1)))
                            g_last = inc(p3b, "tensor", s_t, lambda E, th=th, qbi=qbi, slot=slot: E.matmul(
                                gu_ps[:, slot, 1, :],
                                gu_sl[:, qbi, NKC - 1, 128:256],
                                xn_own[:, NKC - 1, th * 512:(th + 1) * 512],
                                start=False, stop=True))
                            if th == 1:
                                gu_free[qbi] = g_last
                            wait(p3b, "scalar", s_t, g_last)
                            slot_sil[slot] = inc(p3b, "scalar", s_s, lambda E, slot=slot, th=th: E.activation(
                                sil_t[:, th * 512:(th + 1) * 512], gu_ps[:, slot, 0, :], AF.Silu))
                            wait(p3b, "vector", s_s, slot_sil[slot])
                            g_int = inc(p3b, "vector", s_v, lambda E, k=k, th=th, slot=slot: E.tensor_mul(
                                inter[:, k, th * 512:(th + 1) * 512],
                                sil_t[:, th * 512:(th + 1) * 512], gu_ps[:, slot, 1, :]))
                            slot_int[slot] = g_int
                        g_int_prev[k] = g_int
                    wait(p3b, "tensor", s_dp[dbi], ldd)
                    wait(p3b, "tensor", s_v, g_int_prev[3])
                    g_dn_last = 0
                    for t in range(NT):
                        for oh in range(2):
                            pp = (t * 2 + oh) % 2
                            wait(p3b, "tensor", s_v, dn_free[pp])
                            for k in range(4):
                                for o2 in range(2):
                                    if not (k == 3 and o2 == 1):
                                        op(p3b, "tensor", lambda E, k=k, o2=o2, oh=oh, t=t, dbi=dbi, pp=pp: E.matmul(
                                            dn_ps[:, pp, o2 * 512:(o2 + 1) * 512],
                                            inter[:, k, t * 128:(t + 1) * 128],
                                            dp_sl[:, dbi, k, oh * 1024 + o2 * 512: oh * 1024 + (o2 + 1) * 512],
                                            start=(k == 0), stop=(k == 3)))
                            g_dn = inc(p3b, "tensor", s_t, lambda E, oh=oh, t=t, dbi=dbi, pp=pp: E.matmul(
                                dn_ps[:, pp, 512:1024],
                                inter[:, 3, t * 128:(t + 1) * 128],
                                dp_sl[:, dbi, 3, oh * 1024 + 512: oh * 1024 + 1024],
                                start=False, stop=True))
                            g_dn_last = g_dn
                            wait(p3b, "vector", s_t, g_dn)
                            wait(p3b, "vector", s_v)
                            dn_free[pp] = inc(p3b, "vector", s_v, lambda E, t=t, oh=oh, e=e, pp=pp: E.scalar_tensor_tensor(
                                acc[:, t, oh * 1024:(oh + 1) * 1024], dn_ps[:, pp, :],
                                W_sb[:, t, e:e + 1], acc[:, t, oh * 1024:(oh + 1) * 1024],
                                ALU.mult, ALU.add))
                    dp_free[dbi] = g_dn_last
                barrier(p3b)
                run(p3b)

        if upto >= 5:
          with contextlib.ExitStack() as ph:
            lsb = lambda n, s, d: ph.enter_context(nc.sbuf_tensor(n, s, d))
            shd_bf = lsb("shd_bf", [128, 4, HID], BF16)
            inter = lsb("inter_sh", [128, 4, 1024], BF16)
            sil_t = lsb("sil_t2", [128, 1024], F32)
            with (
                nc.psum_tensor("sgu_ps", [128, 2, 1024], F32) as gu_ps,
                nc.psum_tensor("sdn_ps", [128, 2, 1024], F32) as dn_ps,
            ):
                p3c = Phase()
                phase_sync(p3c)
                sh_state["l3"] = dma(p3c, "gpsimd", s_wo2[0], shd_bf[:], shd_img)
                wait(p3c, "tensor", s_w, sh_state["l2"])
                g_int_prev = 0
                for k in range(4):
                    wait(p3c, "tensor", s_v, g_int_prev)
                    for kc in range(NKC):
                        for th in range(2):
                            op(p3c, "tensor", lambda E, kc=kc, th=th, k=k: E.matmul(
                                gu_ps[:, 0, th * 512:(th + 1) * 512],
                                shg_bf[:, kc, k * 128:(k + 1) * 128],
                                xn_own[:, kc, th * 512:(th + 1) * 512],
                                start=(kc == 0), stop=(kc == NKC - 1)))
                            if not (kc == NKC - 1 and th == 1):
                                op(p3c, "tensor", lambda E, kc=kc, th=th, k=k: E.matmul(
                                    gu_ps[:, 1, th * 512:(th + 1) * 512],
                                    shu_bf[:, kc, k * 128:(k + 1) * 128],
                                    xn_own[:, kc, th * 512:(th + 1) * 512],
                                    start=(kc == 0), stop=(kc == NKC - 1)))
                    g_gu = inc(p3c, "tensor", s_t, lambda E, k=k: E.matmul(
                        gu_ps[:, 1, 512:1024],
                        shu_bf[:, NKC - 1, k * 128:(k + 1) * 128],
                        xn_own[:, NKC - 1, 512:1024],
                        start=False, stop=True))
                    wait(p3c, "scalar", s_t, g_gu)
                    g_sil = inc(p3c, "scalar", s_s, lambda E: E.activation(
                        sil_t[:], gu_ps[:, 0, :], AF.Silu))
                    wait(p3c, "vector", s_s, g_sil)
                    g_int = inc(p3c, "vector", s_v, lambda E, k=k: E.tensor_mul(
                        inter[:, k, :], sil_t[:], gu_ps[:, 1, :]))
                    g_int_prev = g_int
                    wait(p3c, "tensor", s_s, g_sil)
                    wait(p3c, "tensor", s_v, g_int)
                dn_free = [0, 0]
                wait(p3c, "tensor", s_wo2[0], sh_state["l3"])
                for t in range(NT):
                    for oh in range(2):
                        pp = (t * 2 + oh) % 2
                        wait(p3c, "tensor", s_v, dn_free[pp])
                        for k in range(4):
                            for o2 in range(2):
                                if not (k == 3 and o2 == 1):
                                    op(p3c, "tensor", lambda E, k=k, o2=o2, oh=oh, t=t, pp=pp: E.matmul(
                                        dn_ps[:, pp, o2 * 512:(o2 + 1) * 512],
                                        inter[:, k, t * 128:(t + 1) * 128],
                                        shd_bf[:, k, oh * 1024 + o2 * 512: oh * 1024 + (o2 + 1) * 512],
                                        start=(k == 0), stop=(k == 3)))
                        g_dn = inc(p3c, "tensor", s_t, lambda E, oh=oh, t=t, pp=pp: E.matmul(
                            dn_ps[:, pp, 512:1024],
                            inter[:, 3, t * 128:(t + 1) * 128],
                            shd_bf[:, 3, oh * 1024 + 512: oh * 1024 + 1024],
                            start=False, stop=True))
                        wait(p3c, "vector", s_t, g_dn)
                        wait(p3c, "vector", s_v)
                        dn_free[pp] = inc(p3c, "vector", s_v, lambda E, t=t, oh=oh, pp=pp: E.scalar_tensor_tensor(
                            acc[:, t, oh * 1024:(oh + 1) * 1024], dn_ps[:, pp, :],
                            shgate_sb[:, t:t + 1], acc[:, t, oh * 1024:(oh + 1) * 1024],
                            ALU.mult, ALU.add))
                barrier(p3c)
                run(p3c)

        with contextlib.ExitStack() as ph:
            x_t2 = ph.enter_context(nc.sbuf_tensor("x_t2c", [128, 2, HID], F32))
            o_t2 = ph.enter_context(nc.sbuf_tensor("o_t2", [128, 2, HID], F32))
            bo_b = ph.enter_context(nc.sbuf_tensor("bo_b", [128, HID], F32))
            p4 = Phase()
            phase_sync(p4)
            if upto < 5:
                seq(p4, "vector", [lambda E: E.memset(acc[:].rearrange("p a b -> p (a b)"), 0.0)])
            ldb = dma(p4, "sync", s_x, bo_b[:], bo_bc)
            wait(p4, "vector", s_x, ldb)
            xf = [0, 0]
            of = [0, 0]
            for t in range(NT):
                bi = t % 2
                wait(p4, "sync", s_v, xf[bi])
                ld = dma(p4, "sync", s_xp[bi], x_t2[:, bi, :], x_own[t * 128:(t + 1) * 128, :])
                wait(p4, "vector", s_xp[bi], ld)
                wait(p4, "vector", s_op[bi], of[bi])
                xf[bi] = seq(p4, "vector", [
                    (lambda t, bi: lambda E: E.tensor_add(
                        o_t2[:, bi, :], acc[:, t, :], x_t2[:, bi, :]))(t, bi),
                    (lambda bi: lambda E: E.tensor_add(
                        o_t2[:, bi, :], o_t2[:, bi, :], bo_b[:]))(bi),
                ])
                wait(p4, "sync", s_v, xf[bi])
                of[bi] = dma(p4, "sync", s_op[bi], out_t[t * 128:(t + 1) * 128, :], o_t2[:, bi, :])
            wait(p4, "sync", s_op[0])
            wait(p4, "sync", s_op[1])
            wait(p4, "vector", s_op[0])
            wait(p4, "vector", s_op[1])
            run(p4)

    return nc


def prep_shared(ins):
    C = np.ascontiguousarray
    f = np.float32
    w = {}
    wqT = ins["L0_attn_wq"].T
    wkT = ins["L0_attn_wk"].T
    wvT = ins["L0_attn_wv"].T
    woT = ins["L0_attn_wo"].T
    img = lambda m: C(np.asarray(m, f).reshape(NKC, 128, NH, 128).transpose(2, 1, 0, 3))
    w["wq_img"] = img(wqT)
    w["wk_img"] = img(wkT)
    w["wv_img"] = img(wvT)
    w["wo_img"] = C(np.asarray(woT, f).reshape(NKC, 128, 4, 512).transpose(2, 1, 0, 3))
    w["bq_r"] = C(np.asarray(ins["L0_attn_bq"], f).reshape(NH, 128).T)
    w["bk_r"] = C(np.asarray(ins["L0_attn_bk"], f).reshape(NH, 128).T)
    w["bv_r"] = C(np.asarray(ins["L0_attn_bv"], f).reshape(NH, 128).T)
    w["bo_bc"] = C(np.broadcast_to(np.asarray(ins["L0_attn_bo"], f), (128, HID)))
    w["wmap_qT"] = C(np.asarray(ins["L0_attn_wmap_q"], f).T)
    w["wmap_kT"] = C(np.asarray(ins["L0_attn_wmap_k"], f).T)
    w["gate_img"] = C(np.asarray(ins["L0_ffn_gate_w"], f).T.reshape(NKC, 128, NE).transpose(1, 0, 2))
    guT = np.asarray(ins["L0_ffn_gu_w"], f).transpose(0, 2, 1)
    gu4 = guT.reshape(NE, HID, 8, 128)[:, :, [0, 4, 1, 5, 2, 6, 3, 7], :]
    gu4 = gu4.reshape(NE, NKC, 128, 4, 256).transpose(0, 3, 2, 1, 4)
    w["gu_img"] = C(gu4)
    dpT = np.asarray(ins["L0_ffn_dp_w"], f).transpose(0, 2, 1)
    w["dp_img"] = C(dpT.reshape(NE, 4, 128, HID).transpose(0, 2, 1, 3))
    shim = lambda m: C(np.asarray(m, f).T.reshape(NKC, 128, NI).transpose(1, 0, 2))
    w["shg_img"] = shim(ins["L0_ffn_sh_gate_w"])
    w["shu_img"] = shim(ins["L0_ffn_sh_up_w"])
    w["shd_img"] = C(np.asarray(ins["L0_ffn_sh_down_w"], f).T.reshape(4, 128, HID).transpose(1, 0, 2))
    w["shg1_img"] = C(np.asarray(ins["L0_ffn_sh_gate1_w"], f).T.reshape(NKC, 128, 1).transpose(1, 0, 2))
    w["ident_bf"] = np.eye(128, dtype=ml_dtypes.bfloat16)
    w["ident_f"] = np.eye(128, dtype=np.float32)
    return w


def make_in_maps(ins):
    shared = prep_shared(ins)
    x = np.asarray(ins["L0_attn_hidden_states"], np.float32)
    maps = []
    for c in range(8):
        b, qd = c // 4, c % 4
        m = dict(shared)
        m["x_full"] = np.ascontiguousarray(x[b])
        m["x_own"] = np.ascontiguousarray(x[b, qd * 1024:(qd + 1) * 1024])
        maps.append(m)
    return maps


def kernel(**inputs):
    maps = make_in_maps(inputs)
    nc = build_nc(debug=False)
    res = run_bass_kernel_spmd(nc, maps, core_ids=list(range(8)))
    out = np.concatenate([res.results[c]["out"] for c in range(8)], axis=0)
    return out.reshape(2, 4096, 2048)
